# version 02
# baseline (speedup 1.0000x reference)
import numpy as np
import ml_dtypes
from contextlib import ExitStack

import concourse.bass as bass
import concourse.mybir as mybir
from concourse.alu_op_type import AluOpType as OP
from concourse.bass_utils import run_bass_kernel_spmd

F32 = mybir.dt.float32
BF16 = mybir.dt.bfloat16
I32 = mybir.dt.int32
AF = mybir.ActivationFunctionType
AX = mybir.AxisListType.X

S = 2048
D = 2048
NT = 16
NH = 16
HEADS = 16
F0 = 5632
NFT = 44
NFC = 11
SCALE = 1.0 / float(np.sqrt(np.float32(128.0)))
NEGBIG = -1.0e30
BF = ml_dtypes.bfloat16
ENGS = ("tensor", "vector", "scalar", "sync", "gpsimd")
DBG_SKIP = set()


def pack_lhsT(wT):
    H, O = wT.shape
    return np.ascontiguousarray(
        wT.reshape(H // 128, 128, O // 128, 128).transpose(2, 1, 0, 3).reshape(O // 128, 128, H))


def pack_rhs(wT):
    H, F = wT.shape
    return np.ascontiguousarray(wT.reshape(H // 128, 128, F))


def make_masks():
    out = np.zeros((128, 4 * 512), dtype=BF)
    r = np.arange(128)[:, None]
    c = np.arange(512)[None, :]
    for oi in range(4):
        out[:, oi * 512:(oi + 1) * 512] = np.where(c <= oi * 128 + r, 0.0, NEGBIG).astype(BF)
    return out


class _MockIns:
    def then_inc(self, *a, **kw):
        return self


class _MockEng:
    def __getattr__(self, name):
        def f(*a, **kw):
            return _MockIns()
        return f


AUTOSEM = {"vector": "sVa", "scalar": "sSa", "gpsimd": "sGa"}
AUTOMAP = {"sV": "sVa", "sS": "sSa", "sG": "sGa"}
_FENCE_SKIP = {"wait_ge", "dma_start", "dma_start_transpose", "indirect_dma_start",
               "nop", "sem_inc", "sem_clear"}


class _Fence:

    def __init__(self, kb, eng, sname):
        self._kb = kb
        self._eng = eng
        self._sname = sname

    def __getattr__(self, a):
        real = getattr(self._eng, a)
        if a in _FENCE_SKIP or not callable(real):
            return real

        def f(*args, **kw):
            kb = self._kb
            sn = self._sname
            kb.sem(sn)
            self._eng.wait_ge(kb.sem_h[sn], kb.cnt.get(sn, 0))
            ins = real(*args, **kw)
            kb.cnt[sn] = kb.cnt.get(sn, 0) + 1
            if not isinstance(ins, _MockIns):
                ins.then_inc(kb.sem_h[sn], 1)
            return ins
        return f


class KB:
    def __init__(self, dbg_upto=None, dbg=False):
        self.nc = bass.Bass("TRN2", target_bir_lowering=False)
        self.dbg = dbg
        self.dbg_upto = 99 if dbg_upto is None else dbg_upto
        self.ctx = ExitStack()
        self.sem_h = {}
        self.cnt = {}
        self.marks = {}
        self.mock = False

    def sem(self, n):
        if n not in self.sem_h:
            self.sem_h[n] = self.ctx.enter_context(self.nc.semaphore(n))
        self.cnt.setdefault(n, 0)
        return self.sem_h[n]

    def bump(self, ins, n, amt=1, mark=None):
        if n in AUTOMAP:
            sn = AUTOMAP[n]
            self.sem(sn)
            v = self.cnt.get(sn, 0)
            if mark is not None:
                self.marks[mark] = (sn, v)
            return v
        self.sem(n)
        self.cnt[n] += amt
        if mark is not None:
            self.marks[mark] = (n, self.cnt[n])
        if ins is not None and not isinstance(ins, _MockIns):
            ins.then_inc(self.sem_h[n], amt)
        return self.cnt[n]

    def wt(self, eng, mark):
        got = self.marks.get(mark)
        if got is None:
            if self.mock:
                return
            raise KeyError(f"unresolved mark {mark}")
        n, v = got
        eng.wait_ge(self.sem(n), v)

    def wtv(self, eng, n, v):
        eng.wait_ge(self.sem(n), v)

    def phase(self, name, bodies):
        def wrap(e, engobj):
            return _Fence(self, engobj, AUTOSEM[e]) if e in AUTOSEM else engobj

        snap = dict(self.cnt)
        self.mock = True
        for e in ENGS:
            if e in bodies:
                bodies[e](wrap(e, _MockEng()))
        mock_end = dict(self.cnt)
        self.cnt = snap
        self.mock = False
        with self.nc.Block(name, no_gpsimd_drain=False) as blk:
            for e in ENGS:
                if e in bodies:
                    body = bodies[e]
                    getattr(blk, e)(lambda eng, b=body, e2=e: b(wrap(e2, eng)))
        for kk2 in set(mock_end) | set(self.cnt):
            assert mock_end.get(kk2, 0) == self.cnt.get(kk2, 0), (
                f"phase {name}: count drift on {kk2}: mock={mock_end.get(kk2, 0)} "
                f"real={self.cnt.get(kk2, 0)}")

    def build(self):
        nc = self.nc
        dbgkind = "ExternalOutput" if self.dbg else "Internal"
        self.x0 = nc.dram_tensor("x0", [S, D], F32, kind="ExternalInput").ap()
        self.wqk0 = nc.dram_tensor("wqk0", [32, 128, D], F32, kind="ExternalInput").ap()
        self.wv0 = nc.dram_tensor("wv0", [16, 128, D], F32, kind="ExternalInput").ap()
        self.wo0 = nc.dram_tensor("wo0", [NH, 128, D], F32, kind="ExternalInput").ap()
        self.w13 = nc.dram_tensor("w13", [NH, 128, 2 * F0], F32, kind="ExternalInput").ap()
        self.w2p = nc.dram_tensor("w2p", [NH, 128, F0], F32, kind="ExternalInput").ap()
        self.nwbi = nc.dram_tensor("nwbi", [128, F0], F32, kind="ExternalInput").ap()
        self.wq1 = nc.dram_tensor("wq1", [NH, 128, D], F32, kind="ExternalInput").ap()
        self.wk1 = nc.dram_tensor("wk1", [1, 128, D], F32, kind="ExternalInput").ap()
        self.wv1 = nc.dram_tensor("wv1", [1, 128, D], F32, kind="ExternalInput").ap()
        self.wo1 = nc.dram_tensor("wo1", [NH, 128, D], F32, kind="ExternalInput").ap()
        self.wg1 = nc.dram_tensor("wg1", [NFT, 128, D], F32, kind="ExternalInput").ap()
        self.wu1 = nc.dram_tensor("wu1", [NFT, 128, D], F32, kind="ExternalInput").ap()
        self.wd1 = nc.dram_tensor("wd1", [NH, 128, F0], F32, kind="ExternalInput").ap()
        self.gsp = nc.dram_tensor("gsp", [128, NFT], F32, kind="ExternalInput").ap()
        self.usp = nc.dram_tensor("usp", [128, NFT], F32, kind="ExternalInput").ap()
        self.dsp = nc.dram_tensor("dsp", [128, NH], F32, kind="ExternalInput").ap()
        self.iden = nc.dram_tensor("iden", [128, 128], BF16, kind="ExternalInput").ap()
        self.mskin = nc.dram_tensor("mskin", [128, 2048], BF16, kind="ExternalInput").ap()
        self.cst = nc.dram_tensor("cst", [128, 8], F32, kind="ExternalInput").ap()

        self.out = nc.dram_tensor("outp", [S, D], F32, kind="ExternalOutput").ap()

        self.codes_h = nc.dram_tensor("codes_h", [NH, 128, S], BF16, kind=dbgkind).ap()
        self.xn_h = nc.dram_tensor("xn_h", [NH, 128, S], BF16, kind=dbgkind).ap()
        self.q_hb = nc.dram_tensor("q_hb", [HEADS, 128, S], BF16, kind=dbgkind).ap()
        self.v_hb = nc.dram_tensor("v_hb", [NT, 128, D], BF16, kind=dbgkind).ap()
        self.s_hb = nc.dram_tensor("s_hb", [NT, 128, D], BF16, kind=dbgkind).ap()
        self.g_hb = nc.dram_tensor("g_hb", [NT, 128, D], BF16, kind=dbgkind).ap()
        self.x1_h = nc.dram_tensor("x1_h", [NT, 128, D], F32, kind=dbgkind).ap()
        self.gn_hb = nc.dram_tensor("gn_hb", [S, F0], BF16, kind=dbgkind).ap()
        self.hf_h = nc.dram_tensor("hf_h", [NFT, 128, S], BF16, kind=dbgkind).ap()

        ctx = self.ctx
        self.P16 = ctx.enter_context(nc.sbuf_tensor("P16", [128, 86016], BF16))
        self.P32 = ctx.enter_context(nc.sbuf_tensor("P32", [128, 8192], F32))
        self.PI = ctx.enter_context(nc.sbuf_tensor("PI", [128, 1024], I32))
        self.ident = ctx.enter_context(nc.sbuf_tensor("identS", [128, 128], BF16))
        self.sinv = ctx.enter_context(nc.sbuf_tensor("sinv", [128, NT], F32))
        self.sinv3 = ctx.enter_context(nc.sbuf_tensor("sinv3", [128, NT], F32))
        self.ssqB = ctx.enter_context(nc.sbuf_tensor("ssqB", [128, NT], F32))
        self.mxgB = ctx.enter_context(nc.sbuf_tensor("mxgB", [128, NT], F32))
        self.aa8 = ctx.enter_context(nc.sbuf_tensor("aa8", [128, NT], F32))
        self.i2v = ctx.enter_context(nc.sbuf_tensor("i2v", [128, NT], F32))
        self.gs_s = ctx.enter_context(nc.sbuf_tensor("gs_s", [128, NFT], F32))
        self.us_s = ctx.enter_context(nc.sbuf_tensor("us_s", [128, NFT], F32))
        self.ds_s = ctx.enter_context(nc.sbuf_tensor("ds_s", [128, NH], F32))
        self.epsb = ctx.enter_context(nc.sbuf_tensor("epsb", [128, 1], F32))
        self.st1 = ctx.enter_context(nc.sbuf_tensor("st1", [128, 8], F32))
        self.st2 = ctx.enter_context(nc.sbuf_tensor("st2", [128, 8], F32))
        self.st3 = ctx.enter_context(nc.sbuf_tensor("st3", [128, 8], F32))

        self.A = self.P16[:, 0:32768]
        self.Bp = self.P16[:, 32768:65536]
        self.Cp = self.P16[:, 65536:86016]

        ph = 1
        self.phase_setup()
        if self.dbg_upto >= ph:
            self.phase_ln(0)
        ph += 1
        if self.dbg_upto >= ph:
            self.phase_qkv(0)
        ph += 1
        if self.dbg_upto >= ph:
            self.phase_att(0)
        ph += 1
        if self.dbg_upto >= ph:
            self.phase_wo(0)
        ph += 1
        if self.dbg_upto >= ph:
            self.phase_ffn0A()
            self.phase_ffn0S()
            self.phase_ffn0B()
        ph += 1
        if self.dbg_upto >= ph:
            self.phase_ln(1)
        ph += 1
        if self.dbg_upto >= ph:
            self.phase_qkv(1)
        ph += 1
        if self.dbg_upto >= ph:
            self.phase_att(1)
        ph += 1
        if self.dbg_upto >= ph:
            self.phase_wo(1)
        ph += 1
        if self.dbg_upto >= ph:
            self.phase_ffn1gu()
            self.phase_ffn1dn()
        ph += 1
        if self.dbg_upto >= ph:
            self.phase_final()
        else:
            self.phase_flush()
        self.ctx.close()
        return self.nc

    def phase_setup(self):
        k = self

        def syncf(sy):
            for dst, src in [(k.ident[:], k.iden), (k.msk(), k.mskin),
                             (k.gs_s[:], k.gsp), (k.us_s[:], k.usp),
                             (k.ds_s[:], k.dsp), (k.st3[:], k.cst)]:
                sy.dma_start(out=dst, in_=src).then_inc(k.sem("ld"), 16)
                k.cnt["ld"] = k.cnt.get("ld", 0) + 16
            k.wtv(sy, "ld", k.cnt.get("ld", 0))
            k.bump(sy.nop(), "sY", mark="setup_done")

        def vecf(v):
            k.wt(v, "setup_done")
            k.bump(v.tensor_copy(k.epsb[:], k.st3[:, 0:1]), "sV", mark="eps_ready")

        self.phase("setup", {"sync": syncf, "vector": vecf})

    def msk(self):
        return self.Cp[:, 18432:20480]

    def phase_ln(self, L):
        k = self
        XF = [k.P32[:, 0:2048], k.P32[:, 2048:4096]]
        XNF = k.P32[:, 4096:6144]
        RS1 = k.P32[:, 6144:7168]
        RS2 = k.P32[:, 7168:8192]
        XN16 = k.Cp[:, 0:2048]
        SC16 = k.Cp[:, 2048:4096]
        CFM = k.Cp[:, 4096:6144]
        SGS = [k.Cp[:, 6144:8192], k.Cp[:, 8192:10240]]
        GGS = [k.Cp[:, 10240:12288], k.Cp[:, 12288:14336]]
        DUMP = k.Cp[:, 14336:16384]
        sums = k.st1[:, 0:1]; ssqv = k.st1[:, 1:2]; mean = k.st1[:, 2:3]
        msq = k.st1[:, 3:4]; varr = k.st1[:, 4:5]; sd = k.st1[:, 5:6]; rstd = k.st1[:, 6:7]
        ssn = k.st1[:, 7:8]
        rmx = k.st2[:, 0:1]; rm2 = k.st2[:, 1:2]; rcp = k.st2[:, 2:3]; rcp127 = k.st2[:, 3:4]
        tmp1 = k.st2[:, 4:5]
        Afm = k.A.rearrange("p (h t) -> p h t", h=NH)

        with self.nc.psum_tensor(f"TP{L}", [128, 2048], BF16) as TP:
            halves = [TP[:, 0:1024], TP[:, 1024:2048]]

            def syncf(sy):
                for i in range(NT):
                    if i >= 2:
                        k.wt(sy, f"ln{L}xfree_{i-2}")
                        if L == 1:
                            k.wtv(sy, "st", k.marks[f"ln{L}stchk_{i-2}"][1]) if False else None
                            k.wt(sy, f"ln{L}stchk_{i-2}")
                    sy.dma_start(out=XF[i % 2][:], in_=k.x0[i * 128:(i + 1) * 128, :]
                                 ).then_inc(k.sem(f"dx{i%2}"), 16)
                    amt = 16
                    if L == 1:
                        sy.dma_start(out=SGS[i % 2][:], in_=k.s_hb[i]).then_inc(k.sem(f"dx{i%2}"), 16)
                        sy.dma_start(out=GGS[i % 2][:], in_=k.g_hb[i]).then_inc(k.sem(f"dx{i%2}"), 16)
                        amt = 48
                    k.bump(None, f"dx{i%2}", amt, mark=f"ln{L}ld_{i}")
                    if L == 1:
                        k.wt(sy, f"ln{L}x1_{i}")
                        sy.dma_start(out=k.x1_h[i], in_=XF[i % 2][:]).then_inc(k.sem("st"), 16)
                        k.bump(None, "st", 16)
                        k.wt(sy, f"ln{L}xnev_{i}_1")
                        sy.dma_start(out=k.xn_h.rearrange("h p t -> p h t")[:, :, i * 128:(i + 1) * 128],
                                     in_=Afm[:, :, i * 128:(i + 1) * 128]).then_inc(k.sem("st"), 16)
                        k.bump(None, "st", 16)
                        k.wtv(sy, "st", k.cnt.get("st", 0))
                        k.bump(sy.nop(), "sY", mark=f"ln{L}stchk_{i}")
                    elif "codes" not in DBG_SKIP:
                        k.wt(sy, f"ln{L}cdev_{i}_1")
                        sy.dma_start(out=k.codes_h.rearrange("h p t -> p h t")[:, :, i * 128:(i + 1) * 128],
                                     in_=CFM.rearrange("p (h t) -> p h t", h=NH)).then_inc(k.sem("st"), 16)
                        k.bump(None, "st", 16)
                        k.wtv(sy, "st", k.cnt.get("st", 0))
                        k.bump(sy.nop(), "sY", mark=f"ln{L}stchk_{i}")
                k.wtv(sy, "st", k.cnt.get("st", 0))
                k.bump(sy.nop(), "sY", mark=f"ln{L}_done")

            def vecf(v):
                for i in range(NT):
                    x = XF[i % 2]
                    k.wt(v, f"ln{L}ld_{i}")
                    if L == 1:
                        v.tensor_tensor(XNF[:], SGS[i % 2][:], GGS[i % 2][:], OP.mult)
                        ins = v.tensor_tensor(x[:], x[:], XNF[:], OP.add)
                        k.bump(ins, "sV", mark=f"ln{L}x1_{i}")
                    ins = v.tensor_reduce(sums, x[:], AX, OP.add)
                    k.bump(ins, "sV", mark=f"ln{L}sum_{i}")
                    k.wt(v, f"ln{L}msq_{i}")
                    v.tensor_tensor(varr, ssn, msq, OP.subtract)
                    ins = v.tensor_tensor(varr, varr, k.epsb[:], OP.add)
                    k.bump(ins, "sV", mark=f"ln{L}var_{i}")
                    k.wt(v, f"ln{L}sd_{i}")
                    ins = v.reciprocal(rstd, sd)
                    k.bump(ins, "sV", mark=f"ln{L}rstd_{i}")
                    k.wt(v, f"ln{L}rstd_{i}")
                    ins = v.tensor_scalar(XNF[:], x[:], mean, rstd, OP.subtract, OP.mult)
                    k.bump(ins, "sV", mark=f"ln{L}xn_{i}")
                    if L == 1:
                        k.marks[f"ln{L}xfree_{i}"] = k.marks[f"ln{L}xn_{i}"]
                    elif "codes" in DBG_SKIP:
                        k.marks[f"ln{L}xfree_{i}"] = k.marks[f"ln{L}xn_{i}"]
                    else:
                        ins = v.tensor_reduce(rmx, XNF[:], AX, OP.max, apply_absolute_value=True)
                        k.bump(ins, "sV", mark=f"ln{L}rmx_{i}")
                        k.wt(v, f"ln{L}rmx_{i}")
                        v.tensor_single_scalar(rm2, rmx, 1e-5, OP.max)
                        ins = v.reciprocal(rcp, rm2)
                        k.bump(ins, "sV", mark=f"ln{L}rcp_{i}")
                        k.wt(v, f"ln{L}rcp_{i}")
                        v.tensor_single_scalar(rcp127, rcp, 127.0, OP.mult)
                        v.tensor_single_scalar(k.sinv[:, i:i + 1], rm2, 1.0 / 127.0, OP.mult)
                        v.tensor_tensor(tmp1, k.sinv[:, i:i + 1], k.sinv[:, i:i + 1], OP.mult)
                        v.tensor_tensor(k.sinv3[:, i:i + 1], tmp1, k.sinv[:, i:i + 1], OP.mult)
                        k.wt(v, f"ln{L}xn16_{i}")
                        for hh in range(2):
                            xnh = XNF[:, hh * 1024:(hh + 1) * 1024]
                            v.tensor_scalar(RS1[:], xnh, rcp127, 0.5, OP.mult, OP.add)
                            v.tensor_copy(k.PI[:], RS1[:])
                            v.tensor_copy(RS2[:], k.PI[:])
                            v.tensor_tensor(xnh, RS2[:], RS1[:], OP.is_gt)
                            ins = v.tensor_tensor(SC16[:, hh * 1024:(hh + 1) * 1024],
                                                  RS2[:], xnh, OP.subtract)
                        k.bump(ins, "sV", mark=f"ln{L}codes_{i}")
                        k.marks[f"ln{L}xfree_{i}"] = k.marks[f"ln{L}codes_{i}"]

            def scaf(sc):
                for i in range(NT):
                    k.wt(sc, f"ln{L}ld_{i}")
                    if L == 1:
                        k.wt(sc, f"ln{L}x1_{i}")
                    ins = sc.activation(DUMP[:], XF[i % 2][:], AF.Square, accum_out=ssqv)
                    k.bump(ins, "sS", mark=f"ln{L}ssq_{i}")
                    k.wt(sc, f"ln{L}sum_{i}")
                    ins = sc.mul(mean, sums, 1.0 / D)
                    k.bump(ins, "sS", mark=f"ln{L}mean_{i}")
                    k.wt(sc, f"ln{L}ssq_{i}")
                    sc.mul(ssn, ssqv, 1.0 / D)
                    k.wt(sc, f"ln{L}mean_{i}")
                    ins = sc.activation(msq, mean, AF.Square)
                    k.bump(ins, "sS", mark=f"ln{L}msq_{i}")
                    k.wt(sc, f"ln{L}var_{i}")
                    ins = sc.activation(sd, varr, AF.Sqrt)
                    k.bump(ins, "sS", mark=f"ln{L}sd_{i}")
                    k.wt(sc, f"ln{L}xn_{i}")
                    ins = sc.copy(XN16[:], XNF[:])
                    k.bump(ins, "sS", mark=f"ln{L}xn16_{i}")
                    if "xpose" not in DBG_SKIP:
                        for b in range(2):
                            k.wt(sc, f"ln{L}txn_{i}_{b}")
                            if "ev3d" in DBG_SKIP:
                                for j in range(8):
                                    ins = sc.copy(k.A[:, (b * 8 + j) * 2048 + i * 128:
                                                     (b * 8 + j) * 2048 + i * 128 + 128],
                                                  halves[b][:, j * 128:(j + 1) * 128])
                            else:
                                dst = Afm[:, b * 8:(b + 1) * 8, i * 128:(i + 1) * 128]
                                ins = sc.copy(dst, halves[b].rearrange("p (h t) -> p h t", h=8))
                            k.bump(ins, "sS", mark=f"ln{L}xnev_{i}_{b}")
                    if L == 0 and "codes" not in DBG_SKIP and "xpose" not in DBG_SKIP:
                        for b in range(2):
                            k.wt(sc, f"ln{L}tcd_{i}_{b}")
                            dst = CFM.rearrange("p (h t) -> p h t", h=NH)[:, b * 8:(b + 1) * 8, :]
                            ins = sc.copy(dst, halves[b].rearrange("p (h t) -> p h t", h=8))
                            k.bump(ins, "sS", mark=f"ln{L}cdev_{i}_{b}")

            def tenf(t):
                if "xpose" in DBG_SKIP:
                    return
                tphist = {}
                for i in range(NT):
                    k.wt(t, f"ln{L}xn16_{i}")
                    for b in range(2):
                        if b in tphist:
                            k.wt(t, tphist[b])
                        for j in range(8):
                            ins = t.transpose(halves[b][:, j * 128:(j + 1) * 128],
                                              XN16[:, (b * 8 + j) * 128:(b * 8 + j) * 128 + 128],
                                              k.ident[:])
                        k.bump(ins, "sT", mark=f"ln{L}txn_{i}_{b}")
                        tphist[b] = f"ln{L}xnev_{i}_{b}"
                    if L == 0 and "codes" not in DBG_SKIP:
                        k.wt(t, f"ln{L}codes_{i}")
                        for b in range(2):
                            k.wt(t, tphist[b])
                            for j in range(8):
                                ins = t.transpose(halves[b][:, j * 128:(j + 1) * 128],
                                                  SC16[:, (b * 8 + j) * 128:(b * 8 + j) * 128 + 128],
                                                  k.ident[:])
                            k.bump(ins, "sT", mark=f"ln{L}tcd_{i}_{b}")
                            tphist[b] = f"ln{L}cdev_{i}_{b}"

            self.phase(f"ln{L}", {"tensor": tenf, "vector": vecf, "scalar": scaf, "sync": syncf})

    def phase_qkv(self, L):
        k = self
        WS = [k.Cp[:, 0:2048], k.Cp[:, 2048:4096]]
        QS = k.Cp[:, 4096:6144]
        VTS = k.Cp[:, 6144:8192]
        if L == 0:
            obs = [("q", i, k.wqk0[i]) for i in range(16)] + \
                  [("k", i, k.wqk0[16 + i]) for i in range(16)] + \
                  [("v", i, k.wv0[i]) for i in range(16)]
        else:
            obs = [("q", i, k.wq1[i]) for i in range(16)] + \
                  [("k", 0, k.wk1[0]), ("v", 0, k.wv1[0])]

        with self.nc.psum_tensor(f"QP{L}", [128, 2048], F32) as PB, \
             self.nc.psum_tensor(f"QT{L}", [128, 2048], BF16) as TP:
            tphalves = [TP[:, 0:1024], TP[:, 1024:2048]]

            def gpsf(g):
                for n, (kind, ob, src) in enumerate(obs):
                    if n >= 2:
                        k.wt(g, f"qkv{L}free_{n-2}")
                    g.dma_start(out=WS[n % 2][:], in_=src).then_inc(k.sem(f"dw{n%2}"), 16)
                    k.bump(None, f"dw{n%2}", 16, mark=f"qkv{L}w_{n}")

            def tenf(t):
                tphist = {}
                for n, (kind, ob, src) in enumerate(obs):
                    k.wt(t, f"qkv{L}w_{n}")
                    for tc in range(4):
                        if n >= 1:
                            k.wt(t, f"qkv{L}ev_{n-1}_{tc}")
                        pq = PB[:, tc * 512:(tc + 1) * 512]
                        for ht in range(NH):
                            ins = t.matmul(pq, WS[n % 2][:, ht * 128:(ht + 1) * 128],
                                           k.A[:, ht * 2048 + tc * 512: ht * 2048 + tc * 512 + 512],
                                           start=(ht == 0), stop=(ht == NH - 1))
                        k.bump(ins, "sT", mark=f"qkv{L}mm_{n}_{tc}")
                    if kind == "v":
                        k.wt(t, f"qkv{L}ev_{n}_3")
                        for half in range(2):
                            if half in tphist:
                                k.wt(t, tphist[half])
                            for j in range(8):
                                tt = half * 8 + j
                                ins = t.transpose(tphalves[half][:, j * 128:(j + 1) * 128],
                                                  QS[:, tt * 128:(tt + 1) * 128], k.ident[:])
                            k.bump(ins, "sT", mark=f"qkv{L}vt_{n}_{half}")
                            tphist[half] = f"qkv{L}vte_{n}_{half}"

            def vecf(v):
                for n, (kind, ob, src) in enumerate(obs):
                    for tc in range(4):
                        k.wt(v, f"qkv{L}mm_{n}_{tc}")
                        pq = PB[:, tc * 512:(tc + 1) * 512]
                        if kind == "q":
                            if n >= 1 and tc == 0:
                                k.wt(v, f"qkv{L}qst_{n-1}") if obs[n - 1][0] == "q" else None
                            ins = v.tensor_single_scalar(QS[:, tc * 512:(tc + 1) * 512], pq, SCALE, OP.mult)
                        elif kind == "k":
                            dst = (k.Bp[:, ob * 2048 + tc * 512: ob * 2048 + tc * 512 + 512]
                                   if L == 0 else k.Bp[:, tc * 512:(tc + 1) * 512])
                            ins = v.tensor_copy(dst, pq)
                        else:
                            if tc == 0:
                                pv = [m for m, (kk, _, _) in enumerate(obs[:n]) if kk in ("q", "v")]
                                if pv:
                                    pn = pv[-1]
                                    if obs[pn][0] == "q":
                                        k.wt(v, f"qkv{L}qst_{pn}")
                                    else:
                                        k.wt(v, f"qkv{L}vt_{pn}_1")
                            ins = v.tensor_copy(QS[:, tc * 512:(tc + 1) * 512], pq)
                        k.bump(ins, "sV", mark=f"qkv{L}ev_{n}_{tc}")
                    k.marks[f"qkv{L}free_{n}"] = k.marks[f"qkv{L}ev_{n}_3"]

            def scaf(sc):
                for n, (kind, ob, src) in enumerate(obs):
                    if kind != "v":
                        continue
                    for half in range(2):
                        k.wt(sc, f"qkv{L}vt_{n}_{half}")
                        prevv = [m for m, (kk, _, _) in enumerate(obs[:n]) if kk == "v"]
                        if prevv:
                            k.wt(sc, f"qkv{L}vstf_{prevv[-1]}_{half}")
                        ins = sc.copy(VTS[:, half * 1024:(half + 1) * 1024], tphalves[half][:])
                        k.bump(ins, "sS", mark=f"qkv{L}vte_{n}_{half}")

            def syncf(sy):
                for n, (kind, ob, src) in enumerate(obs):
                    if kind == "q":
                        k.wt(sy, f"qkv{L}ev_{n}_3")
                        sy.dma_start(out=k.q_hb[ob], in_=QS[:]).then_inc(k.sem("st"), 16)
                        k.bump(None, "st", 16)
                        k.wtv(sy, "st", k.cnt.get("st", 0))
                        k.bump(sy.nop(), "sY", mark=f"qkv{L}qst_{n}")
                    elif kind == "v":
                        for half in range(2):
                            k.wt(sy, f"qkv{L}vte_{n}_{half}")
                            dst = k.v_hb.rearrange("n p t -> p n t")[:, half * 8:(half + 1) * 8,
                                                                     ob * 128:(ob + 1) * 128]
                            srcap = VTS[:, half * 1024:(half + 1) * 1024].rearrange("p (n t) -> p n t", n=8)
                            sy.dma_start(out=dst, in_=srcap).then_inc(k.sem("st"), 16)
                            k.bump(None, "st", 16)
                            k.wtv(sy, "st", k.cnt.get("st", 0))
                            k.bump(sy.nop(), "sY", mark=f"qkv{L}vstf_{n}_{half}")
                k.wtv(sy, "st", k.cnt.get("st", 0))
                k.bump(sy.nop(), "sY", mark=f"qkv{L}_done")

            self.phase(f"qkv{L}", {"tensor": tenf, "vector": vecf, "scalar": scaf,
                                   "sync": syncf, "gpsimd": gpsf})

    def phase_att(self, L):
        k = self
        causal = (L == 0)
        QH = [k.Cp[:, 0:2048], k.Cp[:, 2048:4096]]
        VH = [k.Cp[:, 4096:6144], k.Cp[:, 6144:8192]]
        PSB = k.Cp[:, 8192:10240]
        PT = k.Cp[:, 10240:18432]
        negm = k.st1[:, 0:1]; zz = k.st1[:, 1:2]; rz = k.st1[:, 2:3]

        with self.nc.psum_tensor(f"SC{L}", [128, 2048], F32) as SC, \
             self.nc.psum_tensor(f"PP{L}", [128, 2048], BF16) as PP, \
             self.nc.psum_tensor(f"OT{L}", [128, 512], F32) as OT:
            pph = [PP[:, 0:1024], PP[:, 1024:2048]]

            def nchf(qt):
                return qt // 4 + 1 if causal else 4

            def transp(t, h, qt, ppq):
                nkb = nchf(qt) * 4
                k.wt(t, f"a{L}pn_{h}_{qt}")
                for b in range((nkb + 7) // 8):
                    half = b % 2
                    if half in ppq:
                        k.wt(t, ppq[half])
                    j0, j1 = b * 8, min(nkb, b * 8 + 8)
                    for j in range(j0, j1):
                        ins = t.transpose(pph[half][:, (j - j0) * 128:(j - j0 + 1) * 128],
                                          PSB[:, j * 128:(j + 1) * 128], k.ident[:])
                    k.bump(ins, "sT", mark=f"a{L}tpb_{h}_{qt}_{b}")
                    ppq[half] = f"a{L}pte_{h}_{qt}_{b}"
                k.marks[f"a{L}tp_{h}_{qt}"] = k.marks[f"a{L}tpb_{h}_{qt}_{(nkb-1)//8}"]

            def syncf(sy):
                for h in range(HEADS):
                    if h >= 2:
                        k.wt(sy, f"a{L}hdone_{h-2}")
                    sy.dma_start(out=QH[h % 2][:], in_=k.q_hb[h]).then_inc(k.sem(f"dq{h%2}"), 16)
                    c0 = h * 128 if L == 0 else 0
                    sy.dma_start(out=VH[h % 2][:].rearrange("p (n t) -> p n t", n=16),
                                 in_=k.v_hb.rearrange("n p t -> p n t")[:, :, c0:c0 + 128]
                                 ).then_inc(k.sem(f"dq{h%2}"), 16)
                    k.bump(None, f"dq{h%2}", 32, mark=f"a{L}qv_{h}")

            def tenf(t):
                ppq = {}
                for h in range(HEADS):
                    k.wt(t, f"a{L}qv_{h}")
                    for qt in range(NT):
                        nch = nchf(qt)
                        if qt > 0:
                            k.wt(t, f"a{L}exp_{h}_{qt-1}")
                        elif h > 0:
                            k.wt(t, f"a{L}exp_{h-1}_{NT-1}")
                        for c in range(nch):
                            kbase = h * 2048 if L == 0 else 0
                            ins = t.matmul(SC[:, c * 512:(c + 1) * 512],
                                           QH[h % 2][:, qt * 128:(qt + 1) * 128],
                                           k.Bp[:, kbase + c * 512: kbase + c * 512 + 512],
                                           start=True, stop=True)
                        k.bump(ins, "sT", mark=f"a{L}sc_{h}_{qt}")
                        if qt > 0:
                            transp(t, h, qt - 1, ppq)
                            if (qt - 1) % 4 == 3:
                                self._att_pv(t, L, h, (qt - 1) // 4, causal, VH, PT, OT)
                    transp(t, h, NT - 1, ppq)
                    self._att_pv(t, L, h, 3, causal, VH, PT, OT)

            def vecf(v):
                for h in range(HEADS):
                    for qt in range(NT):
                        nch = nchf(qt)
                        k.wt(v, f"a{L}sc_{h}_{qt}")
                        if causal:
                            off = qt % 4
                            v.tensor_tensor(SC[:, (nch - 1) * 512:nch * 512],
                                            SC[:, (nch - 1) * 512:nch * 512],
                                            k.msk()[:, off * 512:(off + 1) * 512], OP.add)
                        ins = v.tensor_reduce(negm, SC[:, 0:nch * 512], AX, OP.max, negate=True)
                        k.bump(ins, "sV", mark=f"a{L}negm_{h}_{qt}")
                        k.wt(v, f"a{L}exp_{h}_{qt}")
                        ins = v.reciprocal(rz, zz)
                        k.bump(ins, "sV", mark=f"a{L}rz_{h}_{qt}")
                        k.wt(v, f"a{L}rz_{h}_{qt}")
                        ins = v.tensor_scalar_mul(PSB[:, 0:nch * 512], PSB[:, 0:nch * 512], rz)
                        k.bump(ins, "sV", mark=f"a{L}pn_{h}_{qt}")
                        if qt % 4 == 0 and qt >= 4:
                            g = qt // 4 - 1
                            k.wt(v, f"a{L}pv_{h}_{g}")
                            ins = v.tensor_copy(k.A[:, h * 2048 + g * 512: h * 2048 + g * 512 + 512], OT[:])
                            k.bump(ins, "sV", mark=f"a{L}oev_{h}_{g}")
                    k.wt(v, f"a{L}pv_{h}_3")
                    ins = v.tensor_copy(k.A[:, h * 2048 + 3 * 512: h * 2048 + 3 * 512 + 512], OT[:])
                    k.bump(ins, "sV", mark=f"a{L}oev_{h}_3")
                    k.marks[f"a{L}hdone_{h}"] = k.marks[f"a{L}oev_{h}_3"]

            def ptevac(sc, h, qt):
                nkb = nchf(qt) * 4
                if qt % 4 == 0:
                    if qt > 0:
                        k.wt(sc, f"a{L}pv_{h}_{qt//4 - 1}")
                    elif h > 0:
                        k.wt(sc, f"a{L}pv_{h-1}_3")
                for b in range((nkb + 7) // 8):
                    k.wt(sc, f"a{L}tpb_{h}_{qt}_{b}")
                    j0, j1 = b * 8, min(nkb, b * 8 + 8)
                    dst = PT.rearrange("p (n t) -> p n t", n=16)[:, j0:j1,
                                                                 (qt % 4) * 128:(qt % 4) * 128 + 128]
                    src = pph[b % 2][:, 0:(j1 - j0) * 128].rearrange("p (n t) -> p n t", n=j1 - j0)
                    ins = sc.copy(dst, src)
                    k.bump(ins, "sS", mark=f"a{L}pte_{h}_{qt}_{b}")
                k.marks[f"a{L}ptall_{h}_{qt}"] = k.marks[f"a{L}pte_{h}_{qt}_{(nkb-1)//8}"]

            def scaf(sc):
                for h in range(HEADS):
                    for qt in range(NT):
                        nch = nchf(qt)
                        k.wt(sc, f"a{L}negm_{h}_{qt}")
                        if qt > 0:
                            k.wt(sc, f"a{L}tp_{h}_{qt-1}")
                        elif h > 0:
                            k.wt(sc, f"a{L}tp_{h-1}_{NT-1}")
                        ins = sc.activation(PSB[:, 0:nch * 512], SC[:, 0:nch * 512], AF.Exp,
                                            bias=negm, scale=1.0, accum_out=zz)
                        k.bump(ins, "sS", mark=f"a{L}exp_{h}_{qt}")
                        if qt > 0:
                            ptevac(sc, h, qt - 1)
                    ptevac(sc, h, NT - 1)

            self.phase(f"att{L}", {"tensor": tenf, "vector": vecf, "scalar": scaf, "sync": syncf})

    def _att_pv(self, t, L, h, g, causal, VH, PT, OT):
        k = self
        ktn = (4 * g + 4) if causal else 16
        k.wt(t, f"a{L}ptall_{h}_{4*g+3}")
        if not (h == 0 and g == 0):
            k.wt(t, f"a{L}oev_{h}_{g-1}" if g > 0 else f"a{L}oev_{h-1}_3")
        for kt in range(ktn):
            ins = t.matmul(OT[:], VH[h % 2][:, kt * 128:(kt + 1) * 128],
                           PT[:, kt * 512:(kt + 1) * 512],
                           start=(kt == 0), stop=(kt == ktn - 1))
        k.bump(ins, "sT", mark=f"a{L}pv_{h}_{g}")

    def phase_wo(self, L):
        k = self
        wo = k.wo0 if L == 0 else k.wo1
        SST = k.Cp[:, 0:2048]
        with self.nc.psum_tensor(f"WOP{L}", [128, 2048], F32) as PB:
            def gpsf(g):
                for hd in range(NH):
                    g.dma_start(out=k.Bp[:, hd * 2048:(hd + 1) * 2048], in_=wo[hd]
                                ).then_inc(k.sem("cw"), 16)
                    k.cnt["cw"] = k.cnt.get("cw", 0) + 16
                k.wtv(g, "cw", k.cnt.get("cw", 0))
                k.bump(g.nop(), "sGc", mark=f"wo{L}_w")

            def tenf(t):
                k.wt(t, f"wo{L}_w")
                for ts in range(NT):
                    for ec in range(4):
                        idx = ts * 4 + ec
                        if idx >= 4:
                            k.wt(t, f"wo{L}ev_{idx-4}")
                        pq = PB[:, ec * 512:(ec + 1) * 512]
                        for hd in range(NH):
                            ins = t.matmul(pq, k.A[:, hd * 2048 + ts * 128: hd * 2048 + ts * 128 + 128],
                                           k.Bp[:, hd * 2048 + ec * 512: hd * 2048 + ec * 512 + 512],
                                           start=(hd == 0), stop=(hd == NH - 1))
                        k.bump(ins, "sT", mark=f"wo{L}mm_{idx}")

            def vecf(v):
                for ts in range(NT):
                    for ec in range(4):
                        idx = ts * 4 + ec
                        k.wt(v, f"wo{L}mm_{idx}")
                        if ts > 0 and ec == 0:
                            k.wt(v, f"wo{L}sfree_{ts-1}")
                        ins = v.tensor_copy(SST[:, ec * 512:(ec + 1) * 512], PB[:, ec * 512:(ec + 1) * 512])
                        k.bump(ins, "sV", mark=f"wo{L}ev_{idx}")

            def syncf(sy):
                for ts in range(NT):
                    k.wt(sy, f"wo{L}ev_{ts*4+3}")
                    sy.dma_start(out=k.s_hb[ts], in_=SST[:]).then_inc(k.sem("st"), 16)
                    k.bump(None, "st", 16)
                    k.wtv(sy, "st", k.cnt.get("st", 0))
                    k.bump(sy.nop(), "sY", mark=f"wo{L}sfree_{ts}")
                k.bump(sy.nop(), "sY", mark=f"wo{L}_done")

            self.phase(f"wo{L}", {"tensor": tenf, "vector": vecf, "sync": syncf, "gpsimd": gpsf})

    def phase_ffn0A(self):
        k = self
        W1S = [k.A[:, 0:8192], k.A[:, 8192:16384]]
        W3S = [k.A[:, 16384:24576], k.A[:, 24576:32768]]
        NWB = k.Cp[:, 0:5632]
        GNS = [k.Cp[:, 5632:6144], k.Cp[:, 6144:6656]]
        DUMP = k.Cp[:, 6656:7168]
        RL = k.P32[:, 0:512]
        M1 = k.P32[:, 512:1024]
        GT = k.P32[:, 1024:1536]
        ssp = k.st1[:, 0:1]; mxp = k.st1[:, 1:2]

        with self.nc.psum_tensor("F0P", [128, 2048], F32) as PB:
            X1 = [PB[:, 0:512], PB[:, 512:1024]]
            X3 = [PB[:, 1024:1536], PB[:, 1536:2048]]

            def gpsf(g):
                g.dma_start(out=NWB[:], in_=k.nwbi).then_inc(k.sem("cw"), 16)
                k.cnt["cw"] = k.cnt.get("cw", 0) + 16
                k.wtv(g, "cw", k.cnt.get("cw", 0))
                k.bump(g.nop(), "sGc", mark="f0_nwb")
                for fc in range(NFC):
                    if fc >= 2:
                        k.wt(g, f"f0adone_{fc-2}")
                    g.dma_start(out=W1S[fc % 2][:].rearrange("p (h t) -> p h t", h=NH),
                                in_=k.w13[:, :, fc * 512:(fc + 1) * 512].rearrange("h p t -> p h t")
                                ).then_inc(k.sem(f"dw{fc%2}"), 16)
                    g.dma_start(out=W3S[fc % 2][:].rearrange("p (h t) -> p h t", h=NH),
                                in_=k.w13[:, :, F0 + fc * 512:F0 + fc * 512 + 512].rearrange("h p t -> p h t")
                                ).then_inc(k.sem(f"dw{fc%2}"), 16)
                    k.bump(None, f"dw{fc%2}", 32, mark=f"f0w_{fc}")

            def syncf(sy):
                sy.dma_start(out=k.Bp[:].rearrange("p (h t) -> p h t", h=NH),
                             in_=k.codes_h.rearrange("h p t -> p h t")).then_inc(k.sem("ld"), 16)
                k.cnt["ld"] = k.cnt.get("ld", 0) + 16
                k.wtv(sy, "ld", k.cnt.get("ld", 0))
                k.bump(sy.nop(), "sY", mark="f0_codes")
                for fc in range(NFC):
                    for tt in range(NT):
                        n = fc * NT + tt
                        k.wt(sy, f"f0gn_{fc}_{tt}")
                        sy.dma_start(out=k.gn_hb[tt * 128:(tt + 1) * 128, fc * 512:(fc + 1) * 512],
                                     in_=GNS[n % 2][:]).then_inc(k.sem("st"), 16)
                        k.bump(None, "st", 16)
                        k.wtv(sy, "st", k.cnt.get("st", 0))
                        k.bump(sy.nop(), "sY", mark=f"f0gnfree_{fc}_{tt}")
                k.bump(sy.nop(), "sY", mark="f0A_done")

            def tenf(t):
                k.wt(t, "f0_codes")
                for fc in range(NFC):
                    k.wt(t, f"f0w_{fc}")
                    for tt in range(NT):
                        pp = tt % 2
                        n = fc * NT + tt
                        if n >= 2:
                            pf, pt = divmod(n - 2, NT)
                            k.wt(t, f"f0gt_{pf}_{pt}")
                        for ht in range(NH):
                            t.matmul(X1[pp], k.Bp[:, ht * 2048 + tt * 128: ht * 2048 + tt * 128 + 128],
                                     W1S[fc % 2][:, ht * 512:(ht + 1) * 512],
                                     start=(ht == 0), stop=(ht == NH - 1))
                        for ht in range(NH):
                            ins = t.matmul(X3[pp], k.Bp[:, ht * 2048 + tt * 128: ht * 2048 + tt * 128 + 128],
                                           W3S[fc % 2][:, ht * 512:(ht + 1) * 512],
                                           start=(ht == 0), stop=(ht == NH - 1))
                        k.bump(ins, "sT", mark=f"f0mm_{fc}_{tt}")
                    k.marks[f"f0adone_{fc}"] = k.marks[f"f0mm_{fc}_{NT-1}"]

            def scaf(sc):
                for fc in range(NFC):
                    for tt in range(NT):
                        pp = tt % 2
                        n = fc * NT + tt
                        k.wt(sc, f"f0mm_{fc}_{tt}")
                        ins = sc.activation(RL[:], X1[pp], AF.Relu)
                        k.bump(ins, "sS", mark=f"f0rl_{fc}_{tt}")
                        k.wt(sc, f"f0gt_{fc}_{tt}")
                        if n >= 1:
                            pf, pt = divmod(n - 1, NT)
                            k.wt(sc, f"f0ssacc_{pf}_{pt}")
                        ins = sc.activation(DUMP[:], GT[:], AF.Square, accum_out=ssp)
                        k.bump(ins, "sS", mark=f"f0ss_{fc}_{tt}")

            def vecf(v):
                k.wt(v, "f0_nwb")
                for fc in range(NFC):
                    for tt in range(NT):
                        pp = tt % 2
                        n = fc * NT + tt
                        k.wt(v, f"f0rl_{fc}_{tt}")
                        v.tensor_tensor(M1[:], X1[pp], RL[:], OP.mult)
                        ins = v.scalar_tensor_tensor(GT[:], M1[:], k.sinv3[:, tt:tt + 1], X3[pp],
                                                     OP.mult, OP.mult)
                        k.bump(ins, "sV", mark=f"f0gt_{fc}_{tt}")
                        if n >= 2:
                            pf, pt = divmod(n - 2, NT)
                            k.wt(v, f"f0gnfree_{pf}_{pt}")
                        ins = v.tensor_tensor(GNS[n % 2][:], GT[:], NWB[:, fc * 512:(fc + 1) * 512], OP.mult)
                        k.bump(ins, "sV", mark=f"f0gn_{fc}_{tt}")
                        ins = v.tensor_reduce(mxp, GNS[n % 2][:], AX, OP.max, apply_absolute_value=True)
                        k.bump(ins, "sV", mark=f"f0mxr_{fc}_{tt}")
                        k.wt(v, f"f0ss_{fc}_{tt}")
                        k.wt(v, f"f0mxr_{fc}_{tt}")
                        if fc == 0:
                            v.tensor_copy(k.ssqB[:, tt:tt + 1], ssp)
                            ins = v.tensor_copy(k.mxgB[:, tt:tt + 1], mxp)
                        else:
                            v.tensor_tensor(k.ssqB[:, tt:tt + 1], k.ssqB[:, tt:tt + 1], ssp, OP.add)
                            ins = v.tensor_tensor(k.mxgB[:, tt:tt + 1], k.mxgB[:, tt:tt + 1], mxp, OP.max)
                        k.bump(ins, "sV", mark=f"f0ssacc_{fc}_{tt}")

            self.phase("ffn0A", {"tensor": tenf, "vector": vecf, "scalar": scaf,
                                 "sync": syncf, "gpsimd": gpsf})

    def phase_ffn0S(self):
        k = self
        SD16 = k.P32[:, 0:16]
        RSQ = k.P32[:, 16:32]
        CLP = k.P32[:, 32:48]
        REC = k.P32[:, 48:64]

        def scaf(sc):
            k.wt(sc, "f0A_vecdone")
            ins = sc.activation(SD16[:], k.ssqB[:], AF.Sqrt, bias=k.epsb[:], scale=1.0 / F0)
            k.bump(ins, "sS", mark="f0_sd")

        def vecf(v):
            k.bump(v.nop(), "sV", mark="f0A_vecdone")
            k.wt(v, "f0_sd")
            ins = v.reciprocal(RSQ[:], SD16[:])
            k.bump(ins, "sV", mark="f0_rsq")
            k.wt(v, "f0_rsq")
            v.tensor_tensor(CLP[:], RSQ[:], k.mxgB[:], OP.mult)
            v.tensor_single_scalar(CLP[:], CLP[:], 1e-5, OP.max)
            ins = v.reciprocal(REC[:], CLP[:])
            k.bump(ins, "sV", mark="f0_rec")
            k.wt(v, "f0_rec")
            v.tensor_single_scalar(k.i2v[:], CLP[:], 1.0 / 127.0, OP.mult)
            v.tensor_tensor(k.aa8[:], RSQ[:], REC[:], OP.mult)
            ins = v.tensor_single_scalar(k.aa8[:], k.aa8[:], 127.0, OP.mult)
            k.bump(ins, "sV", mark="f0_stats")

        self.phase("ffn0S", {"scalar": scaf, "vector": vecf})

    def phase_ffn0B(self):
        k = self
        iq2 = [k.Bp[:, ft * 1024:(ft + 1) * 1024] for ft in range(32)] + \
              [k.A[:, 20480 + (ft - 32) * 1024: 20480 + (ft - 31) * 1024] for ft in range(32, NFT)]
        GNL = [k.Cp[:, 0:512], k.Cp[:, 512:1024]]
        IQS = k.Cp[:, 1024:1536]
        GF = k.Cp[:, 1536:2560]
        GS = k.Cp[:, 2560:3584]
        W2L = [k.A[:, 0:5632], k.A[:, 5632:11264]]
        YY = k.P32[:, 0:512]
        FF = k.P32[:, 512:1024]
        CMP = k.P32[:, 1024:1536]

        with self.nc.psum_tensor("F0D", [128, 2048], F32) as DN, \
             self.nc.psum_tensor("F0T", [128, 2048], BF16) as TP, \
             self.nc.psum_tensor("F0T2", [128, 1024], BF16) as TP2:
            tph = [TP[:, 0:1024], TP[:, 1024:2048]]

            def syncf(sy):
                for tb in range(2):
                    for tt8 in range(8):
                        for fc in range(NFC):
                            n = (tb * 8 + tt8) * NFC + fc
                            if n >= 2:
                                k.wt(sy, f"f0y_{n-2}")
                            tt = tb * 8 + tt8
                            sy.dma_start(out=GNL[n % 2][:],
                                         in_=k.gn_hb[tt * 128:(tt + 1) * 128, fc * 512:(fc + 1) * 512]
                                         ).then_inc(k.sem(f"dx{n%2}"), 16)
                            k.bump(None, f"dx{n%2}", 16, mark=f"f0gnl_{n}")
                    for dblk in range(NH):
                        k.wt(sy, f"f0gse_{tb}_{dblk}")
                        for b in range(8):
                            tsub = tb * 8 + b
                            sy.dma_start(out=k.g_hb[tsub][:, dblk * 128:(dblk + 1) * 128],
                                         in_=GS[:, b * 128:(b + 1) * 128]).then_inc(k.sem("st"), 16)
                            k.bump(None, "st", 16)
                        k.wtv(sy, "st", k.cnt.get("st", 0))
                        k.bump(sy.nop(), "sY", mark=f"f0gsfree_{tb}_{dblk}")
                k.wtv(sy, "st", k.cnt.get("st", 0))
                k.bump(sy.nop(), "sY", mark="f0B_done")

            def gpsf(g):
                for tb in range(2):
                    for dblk in range(NH):
                        n = tb * NH + dblk
                        if n >= 2:
                            pn = n - 2
                            k.wt(g, f"f0dn_{pn//NH}_{pn%NH}")
                        g.dma_start(out=W2L[n % 2][:], in_=k.w2p[dblk]).then_inc(k.sem(f"dw{n%2}"), 16)
                        k.bump(None, f"dw{n%2}", 16, mark=f"f0w2_{tb}_{dblk}")

            def vecf(v):
                for tb in range(2):
                    for tt8 in range(8):
                        tt = tb * 8 + tt8
                        for fc in range(NFC):
                            n = (tb * 8 + tt8) * NFC + fc
                            k.wt(v, f"f0gnl_{n}")
                            v.tensor_scalar(YY[:], GNL[n % 2][:], k.aa8[:, tt:tt + 1], 0.5,
                                            OP.mult, OP.add)
                            ins = v.tensor_copy(k.PI[:, 0:512], YY[:])
                            k.bump(ins, "sV", mark=f"f0y_{n}")
                            v.tensor_copy(FF[:], k.PI[:, 0:512])
                            v.tensor_tensor(CMP[:], FF[:], YY[:], OP.is_gt)
                            v.tensor_tensor(FF[:], FF[:], CMP[:], OP.subtract)
                            if n >= 1:
                                k.wt(v, f"f0tiq_{n-1}")
                            ins = v.tensor_scalar_mul(IQS[:], FF[:], k.i2v[:, tt:tt + 1])
                            k.bump(ins, "sV", mark=f"f0iq_{n}")

            def tenf(t):
                for tb in range(2):
                    for tt8 in range(8):
                        for fc in range(NFC):
                            n = (tb * 8 + tt8) * NFC + fc
                            k.wt(t, f"f0iq_{n}")
                            if n >= 2:
                                k.wt(t, f"f0iqe_{n-2}")
                            for b in range(4):
                                ins = t.transpose(tph[n % 2][:, b * 128:(b + 1) * 128],
                                                  IQS[:, b * 128:(b + 1) * 128], k.ident[:])
                            k.bump(ins, "sT", mark=f"f0tiq_{n}")
                    for dblk in range(NH):
                        nn = tb * NH + dblk
                        k.wt(t, f"f0w2_{tb}_{dblk}")
                        k.wt(t, f"f0iqe_{(tb*8+7)*NFC + NFC - 1}")
                        if nn >= 2:
                            pn = nn - 2
                            k.wt(t, f"f0sil_{pn//NH}_{pn%NH}")
                        half = nn % 2
                        for ft in range(NFT):
                            for tc in range(2):
                                ins = t.matmul(DN[:, half * 1024 + tc * 512: half * 1024 + tc * 512 + 512],
                                               W2L[nn % 2][:, ft * 128:(ft + 1) * 128],
                                               iq2[ft][:, tc * 512:(tc + 1) * 512],
                                               start=(ft == 0), stop=(ft == NFT - 1))
                        k.bump(ins, "sT", mark=f"f0dn_{tb}_{dblk}")
                        k.wt(t, f"f0sil_{tb}_{dblk}")
                        if nn >= 1:
                            pn = nn - 1
                            k.wt(t, f"f0gte_{pn//NH}_{pn%NH}")
                        for b in range(8):
                            ins = t.transpose(TP2[:, (b % 8) * 128:(b % 8) * 128 + 128],
                                              GF[:, b * 128:(b + 1) * 128], k.ident[:])
                        k.bump(ins, "sT", mark=f"f0tg_{tb}_{dblk}")

            def scaf(sc):
                for tb in range(2):
                    for tt8 in range(8):
                        tt = tb * 8 + tt8
                        for fc in range(NFC):
                            n = (tb * 8 + tt8) * NFC + fc
                            k.wt(sc, f"f0tiq_{n}")
                            if tb == 1:
                                k.wt(sc, "f0dn_0_15")
                            ins = None
                            for b in range(4):
                                ft = fc * 4 + b
                                ins = sc.copy(iq2[ft][:, tt8 * 128:(tt8 + 1) * 128],
                                              tph[n % 2][:, b * 128:(b + 1) * 128])
                            k.bump(ins, "sS", mark=f"f0iqe_{n}")
                    for dblk in range(NH):
                        nn = tb * NH + dblk
                        k.wt(sc, f"f0dn_{tb}_{dblk}")
                        half = nn % 2
                        ins = sc.activation(GF[:], DN[:, half * 1024:half * 1024 + 1024], AF.Silu)
                        k.bump(ins, "sS", mark=f"f0sil_{tb}_{dblk}")
                        k.wt(sc, f"f0tg_{tb}_{dblk}")
                        if nn >= 1:
                            pn = nn - 1
                            k.wt(sc, f"f0gsfree_{pn//NH}_{pn%NH}")
                        ins = sc.copy(GS[:], TP2[:])
                        k.bump(ins, "sS", mark=f"f0gte_{tb}_{dblk}")
                        k.marks[f"f0gse_{tb}_{dblk}"] = k.marks[f"f0gte_{tb}_{dblk}"]

            self.phase("ffn0B", {"tensor": tenf, "vector": vecf, "scalar": scaf,
                                 "sync": syncf, "gpsimd": gpsf})

    def phase_ffn1gu(self):
        k = self
        WG = [k.Cp[:, 0:2048], k.Cp[:, 2048:4096]]
        WU = [k.Cp[:, 4096:6144], k.Cp[:, 6144:8192]]
        GG = k.P32[:, 0:2048]
        UU = k.P32[:, 2048:4096]
        HF = k.Cp[:, 8192:10240]

        with self.nc.psum_tensor("F1G", [128, 2048], F32) as GP, \
             self.nc.psum_tensor("F1U", [128, 2048], F32) as UP:
            def syncf(sy):
                sy.dma_start(out=k.Bp[:].rearrange("p (h t) -> p h t", h=NH),
                             in_=k.xn_h.rearrange("h p t -> p h t")).then_inc(k.sem("ld"), 16)
                k.cnt["ld"] = k.cnt.get("ld", 0) + 16
                k.wtv(sy, "ld", k.cnt.get("ld", 0))
                k.bump(sy.nop(), "sY", mark="f1_xn")
                for ft in range(NFT):
                    k.wt(sy, f"f1h_{ft}")
                    sy.dma_start(out=k.hf_h[ft], in_=HF[:]).then_inc(k.sem("st"), 16)
                    k.bump(None, "st", 16)
                    k.wtv(sy, "st", k.cnt.get("st", 0))
                    k.bump(sy.nop(), "sY", mark=f"f1hfree_{ft}")
                k.bump(sy.nop(), "sY", mark="f1gu_done")

            def gpsf(g):
                for ft in range(NFT):
                    if ft >= 2:
                        k.wt(g, f"f1h_{ft-2}")
                    g.dma_start(out=WG[ft % 2][:], in_=k.wg1[ft]).then_inc(k.sem(f"dw{ft%2}"), 16)
                    g.dma_start(out=WU[ft % 2][:], in_=k.wu1[ft]).then_inc(k.sem(f"dw{ft%2}"), 16)
                    k.bump(None, f"dw{ft%2}", 32, mark=f"f1w_{ft}")

            def tenf(t):
                k.wt(t, "f1_xn")
                for ft in range(NFT):
                    k.wt(t, f"f1w_{ft}")
                    if ft >= 1:
                        k.wt(t, f"f1ue_{ft-1}")
                        k.wt(t, f"f1gg_{ft-1}")
                    for ht in range(NH):
                        for tc in range(4):
                            t.matmul(GP[:, tc * 512:(tc + 1) * 512],
                                     WG[ft % 2][:, ht * 128:(ht + 1) * 128],
                                     k.Bp[:, ht * 2048 + tc * 512: ht * 2048 + tc * 512 + 512],
                                     start=(ht == 0), stop=(ht == NH - 1))
                    for ht in range(NH):
                        for tc in range(4):
                            ins = t.matmul(UP[:, tc * 512:(tc + 1) * 512],
                                           WU[ft % 2][:, ht * 128:(ht + 1) * 128],
                                           k.Bp[:, ht * 2048 + tc * 512: ht * 2048 + tc * 512 + 512],
                                           start=(ht == 0), stop=(ht == NH - 1))
                    k.bump(ins, "sT", mark=f"f1mm_{ft}")

            def scaf(sc):
                for ft in range(NFT):
                    k.wt(sc, f"f1mm_{ft}")
                    if ft >= 1:
                        k.wt(sc, f"f1h_{ft-1}")
                    ins = sc.activation(GG[:], GP[:], AF.Silu, scale=k.gs_s[:, ft:ft + 1])
                    k.bump(ins, "sS", mark=f"f1gg_{ft}")

            def vecf(v):
                for ft in range(NFT):
                    k.wt(v, f"f1mm_{ft}")
                    ins = v.tensor_scalar_mul(UU[:], UP[:], k.us_s[:, ft:ft + 1])
                    k.bump(ins, "sV", mark=f"f1ue_{ft}")
                    k.wt(v, f"f1gg_{ft}")
                    if ft >= 1:
                        k.wt(v, f"f1hfree_{ft-1}")
                    ins = v.tensor_tensor(HF[:], GG[:], UU[:], OP.mult)
                    k.bump(ins, "sV", mark=f"f1h_{ft}")

            self.phase("ffn1gu", {"tensor": tenf, "vector": vecf, "scalar": scaf,
                                  "sync": syncf, "gpsimd": gpsf})

    def phase_ffn1dn(self):
        k = self
        hfm = [k.Bp[:, ft * 1024:(ft + 1) * 1024] for ft in range(32)] + \
              [k.A[:, 20480 + (ft - 32) * 1024: 20480 + (ft - 31) * 1024] for ft in range(32, NFT)]
        WD = [k.A[:, 0:5632], k.A[:, 5632:11264]]
        GF = k.Cp[:, 0:1024]
        GS = k.Cp[:, 1024:2048]

        with self.nc.psum_tensor("F1D", [128, 2048], F32) as DN, \
             self.nc.psum_tensor("F1T", [128, 1024], BF16) as TP:
            def syncf(sy):
                for tb in range(2):
                    if tb == 1:
                        k.wt(sy, "f1dn_0_15")
                    for ft in range(NFT):
                        n = tb * NFT + ft
                        sy.dma_start(out=hfm[ft][:], in_=k.hf_h[ft][:, tb * 1024:(tb + 1) * 1024]
                                     ).then_inc(k.sem(f"dx{n%2}"), 16)
                        k.bump(None, f"dx{n%2}", 16)
                    k.wtv(sy, "dx0", k.cnt.get("dx0", 0))
                    k.wtv(sy, "dx1", k.cnt.get("dx1", 0))
                    k.bump(sy.nop(), "sY", mark=f"f1hl_all_{tb}")
                    for dblk in range(NH):
                        k.wt(sy, f"f1gte_{tb}_{dblk}")
                        for b in range(8):
                            tsub = tb * 8 + b
                            sy.dma_start(out=k.g_hb[tsub][:, dblk * 128:(dblk + 1) * 128],
                                         in_=GS[:, b * 128:(b + 1) * 128]).then_inc(k.sem("st"), 16)
                            k.bump(None, "st", 16)
                        k.wtv(sy, "st", k.cnt.get("st", 0))
                        k.bump(sy.nop(), "sY", mark=f"f1gsfree_{tb}_{dblk}")
                k.wtv(sy, "st", k.cnt.get("st", 0))
                k.bump(sy.nop(), "sY", mark="f1dn_done")

            def gpsf(g):
                for tb in range(2):
                    for dblk in range(NH):
                        n = tb * NH + dblk
                        if n >= 2:
                            pn = n - 2
                            k.wt(g, f"f1dn_{pn//NH}_{pn%NH}")
                        g.dma_start(out=WD[n % 2][:], in_=k.wd1[dblk]).then_inc(k.sem(f"dw{n%2}"), 16)
                        k.bump(None, f"dw{n%2}", 16, mark=f"f1wd_{tb}_{dblk}")

            def tenf(t):
                for tb in range(2):
                    k.wt(t, f"f1hl_all_{tb}")
                    for dblk in range(NH):
                        nn = tb * NH + dblk
                        k.wt(t, f"f1wd_{tb}_{dblk}")
                        if nn >= 2:
                            pn = nn - 2
                            k.wt(t, f"f1sil_{pn//NH}_{pn%NH}")
                        half = nn % 2
                        for ft in range(NFT):
                            for tc in range(2):
                                ins = t.matmul(DN[:, half * 1024 + tc * 512: half * 1024 + tc * 512 + 512],
                                               WD[nn % 2][:, ft * 128:(ft + 1) * 128],
                                               hfm[ft][:, tc * 512:(tc + 1) * 512],
                                               start=(ft == 0), stop=(ft == NFT - 1))
                        k.bump(ins, "sT", mark=f"f1dn_{tb}_{dblk}")
                        k.wt(t, f"f1sil_{tb}_{dblk}")
                        if nn >= 1:
                            pn = nn - 1
                            k.wt(t, f"f1gte_{pn//NH}_{pn%NH}")
                        for b in range(8):
                            ins = t.transpose(TP[:, b * 128:(b + 1) * 128],
                                              GF[:, b * 128:(b + 1) * 128], k.ident[:])
                        k.bump(ins, "sT", mark=f"f1tg_{tb}_{dblk}")

            def scaf(sc):
                for tb in range(2):
                    for dblk in range(NH):
                        nn = tb * NH + dblk
                        k.wt(sc, f"f1dn_{tb}_{dblk}")
                        half = nn % 2
                        ins = sc.activation(GF[:], DN[:, half * 1024:half * 1024 + 1024],
                                            AF.Silu, scale=k.ds_s[:, dblk:dblk + 1])
                        k.bump(ins, "sS", mark=f"f1sil_{tb}_{dblk}")
                        k.wt(sc, f"f1tg_{tb}_{dblk}")
                        if nn >= 1:
                            pn = nn - 1
                            k.wt(sc, f"f1gsfree_{pn//NH}_{pn%NH}")
                        ins = sc.copy(GS[:], TP[:])
                        k.bump(ins, "sS", mark=f"f1gte_{tb}_{dblk}")

            self.phase("ffn1dn", {"tensor": tenf, "scalar": scaf, "sync": syncf, "gpsimd": gpsf})

    def phase_final(self):
        k = self
        XF = [k.P32[:, 0:2048], k.P32[:, 2048:4096]]
        SGF = k.P32[:, 4096:6144]
        SGS = [k.Cp[:, 0:2048], k.Cp[:, 2048:4096]]
        GGS = [k.Cp[:, 4096:6144], k.Cp[:, 6144:8192]]

        def syncf(sy):
            for i in range(NT):
                sy.dma_start(out=XF[i % 2][:], in_=k.x1_h[i]).then_inc(k.sem(f"dx{i%2}"), 16)
                sy.dma_start(out=SGS[i % 2][:], in_=k.s_hb[i]).then_inc(k.sem(f"dx{i%2}"), 16)
                sy.dma_start(out=GGS[i % 2][:], in_=k.g_hb[i]).then_inc(k.sem(f"dx{i%2}"), 16)
                k.bump(None, f"dx{i%2}", 48, mark=f"fin_ld_{i}")
                k.wt(sy, f"fin_x2_{i}")
                sy.dma_start(out=k.out[i * 128:(i + 1) * 128, :], in_=XF[i % 2][:]
                             ).then_inc(k.sem("out"), 16)
                k.cnt["out"] = k.cnt.get("out", 0) + 16
                k.wtv(sy, "out", k.cnt.get("out", 0))
            k.wtv(sy, "out", k.cnt.get("out", 0))

        def vecf(v):
            for i in range(NT):
                k.wt(v, f"fin_ld_{i}")
                v.tensor_tensor(SGF[:], SGS[i % 2][:], GGS[i % 2][:], OP.mult)
                ins = v.tensor_tensor(XF[i % 2][:], XF[i % 2][:], SGF[:], OP.add)
                k.bump(ins, "sV", mark=f"fin_x2_{i}")

        self.phase("final", {"vector": vecf, "sync": syncf})

    def phase_flush(self):
        k = self

        def syncf(sy):
            if k.cnt.get("st", 0):
                k.wtv(sy, "st", k.cnt.get("st", 0))

        self.phase("flush", {"sync": syncf})


def _prep_inputs(ins):
    f32 = np.float32
    wqkvT = np.ascontiguousarray(ins["L0_attn_w_qkv"].astype(f32).T)
    woT0 = np.ascontiguousarray(ins["L0_attn_w_o"].astype(f32).T)
    w13T = np.ascontiguousarray(ins["L0_ffn_w13"].astype(f32).T)
    w2T = np.ascontiguousarray(ins["L0_ffn_w2"].astype(f32).T)
    qwT = np.ascontiguousarray(ins["L1_attn_q_w"].astype(f32).T)
    kwT = np.ascontiguousarray(ins["L1_attn_k_w"].astype(f32).T)
    vwT = np.ascontiguousarray(ins["L1_attn_v_w"].astype(f32).T)
    owT = np.ascontiguousarray(ins["L1_attn_o_w"].astype(f32).T)
    gwT = np.ascontiguousarray(ins["L1_ffn_gate_w"].astype(f32).T)
    uwT = np.ascontiguousarray(ins["L1_ffn_up_w"].astype(f32).T)
    dwT = np.ascontiguousarray(ins["L1_ffn_down_w"].astype(f32).T)

    d = {}
    d["wqk0"] = pack_lhsT(wqkvT[:, 0:4096])
    d["wv0"] = pack_lhsT(wqkvT[:, 4096:6144])
    d["wo0"] = pack_rhs(woT0)
    d["w13"] = pack_rhs(w13T)
    d["w2p"] = pack_lhsT(w2T)
    d["nwbi"] = np.ascontiguousarray(
        np.broadcast_to(ins["L0_ffn_norm_w"].astype(f32)[None, :], (128, F0)))
    d["wq1"] = pack_lhsT(qwT)
    d["wk1"] = pack_lhsT(kwT)
    d["wv1"] = pack_lhsT(vwT)
    d["wo1"] = pack_rhs(owT)
    d["wg1"] = pack_lhsT(gwT)
    d["wu1"] = pack_lhsT(uwT)
    d["wd1"] = pack_lhsT(dwT)
    d["gsp"] = np.ascontiguousarray(ins["L1_ffn_gate_s"].astype(f32).reshape(NFT, 128).T)
    d["usp"] = np.ascontiguousarray(ins["L1_ffn_up_s"].astype(f32).reshape(NFT, 128).T)
    d["dsp"] = np.ascontiguousarray(ins["L1_ffn_down_s"].astype(f32).reshape(NH, 128).T)
    d["iden"] = np.eye(128, dtype=BF)
    d["mskin"] = make_masks()
    cst = np.zeros((128, 8), np.float32)
    cst[:, 0] = 1e-5
    d["cst"] = cst
    return d


def kernel(**inputs):
    shared = _prep_inputs(inputs)
    x_all = np.ascontiguousarray(inputs["L0_attn_hidden_states"].astype(np.float32))
    kb = KB()
    nc = kb.build()
    in_maps = []
    for b in range(8):
        m = dict(shared)
        m["x0"] = x_all[b]
        in_maps.append(m)
    res = run_bass_kernel_spmd(nc, in_maps, core_ids=list(range(8)))
    return np.stack([r["outp"] for r in res.results], axis=0).astype(np.float32)


# version 03 (final)
# speedup vs baseline: 1.0159x; 1.0159x over previous
import numpy as np
import ml_dtypes
from contextlib import ExitStack

import concourse.bass as bass
import concourse.mybir as mybir
from concourse.alu_op_type import AluOpType as OP
from concourse.bass_utils import run_bass_kernel_spmd

F32 = mybir.dt.float32
BF16 = mybir.dt.bfloat16
I32 = mybir.dt.int32
AF = mybir.ActivationFunctionType
AX = mybir.AxisListType.X

S = 2048
D = 2048
NT = 16
NH = 16
HEADS = 16
F0 = 5632
NFT = 44
NFC = 11
SCALE = 1.0 / float(np.sqrt(np.float32(128.0)))
NEGBIG = -1.0e30
BF = ml_dtypes.bfloat16
ENGS = ("tensor", "vector", "scalar", "sync", "gpsimd")
DBG_SKIP = set()


def pack_lhsT(wT):
    H, O = wT.shape
    return np.ascontiguousarray(
        wT.reshape(H // 128, 128, O // 128, 128).transpose(2, 1, 0, 3).reshape(O // 128, 128, H))


def pack_rhs(wT):
    H, F = wT.shape
    return np.ascontiguousarray(wT.reshape(H // 128, 128, F))


def make_masks():
    out = np.zeros((128, 4 * 512), dtype=BF)
    r = np.arange(128)[:, None]
    c = np.arange(512)[None, :]
    for oi in range(4):
        out[:, oi * 512:(oi + 1) * 512] = np.where(c <= oi * 128 + r, 0.0, NEGBIG).astype(BF)
    return out


class _MockIns:
    def then_inc(self, *a, **kw):
        return self


class _MockEng:
    def __getattr__(self, name):
        def f(*a, **kw):
            return _MockIns()
        return f


AUTOSEM = {"vector": "sVa", "scalar": "sSa", "gpsimd": "sGa"}
AUTOMAP = {"sV": "sVa", "sS": "sSa", "sG": "sGa"}
_FENCE_SKIP = {"wait_ge", "dma_start", "dma_start_transpose", "indirect_dma_start",
               "nop", "sem_inc", "sem_clear"}


class _Fence:

    def __init__(self, kb, eng, sname):
        self._kb = kb
        self._eng = eng
        self._sname = sname

    def __getattr__(self, a):
        real = getattr(self._eng, a)
        if a in _FENCE_SKIP or not callable(real):
            return real

        def f(*args, **kw):
            kb = self._kb
            sn = self._sname
            kb.sem(sn)
            self._eng.wait_ge(kb.sem_h[sn], kb.cnt.get(sn, 0))
            ins = real(*args, **kw)
            kb.cnt[sn] = kb.cnt.get(sn, 0) + 1
            if not isinstance(ins, _MockIns):
                ins.then_inc(kb.sem_h[sn], 1)
            return ins
        return f


class KB:
    def __init__(self, dbg_upto=None, dbg=False):
        self.nc = bass.Bass("TRN2", target_bir_lowering=False)
        self.dbg = dbg
        self.dbg_upto = 99 if dbg_upto is None else dbg_upto
        self.ctx = ExitStack()
        self.sem_h = {}
        self.cnt = {}
        self.marks = {}
        self.mock = False

    def sem(self, n):
        if n not in self.sem_h:
            self.sem_h[n] = self.ctx.enter_context(self.nc.semaphore(n))
        self.cnt.setdefault(n, 0)
        return self.sem_h[n]

    def bump(self, ins, n, amt=1, mark=None):
        if n in AUTOMAP:
            sn = AUTOMAP[n]
            self.sem(sn)
            v = self.cnt.get(sn, 0)
            if mark is not None:
                self.marks[mark] = (sn, v)
            return v
        self.sem(n)
        self.cnt[n] += amt
        if mark is not None:
            self.marks[mark] = (n, self.cnt[n])
        if ins is not None and not isinstance(ins, _MockIns):
            ins.then_inc(self.sem_h[n], amt)
        return self.cnt[n]

    def wt(self, eng, mark):
        got = self.marks.get(mark)
        if got is None:
            if self.mock:
                return
            raise KeyError(f"unresolved mark {mark}")
        n, v = got
        eng.wait_ge(self.sem(n), v)

    def wtv(self, eng, n, v):
        eng.wait_ge(self.sem(n), v)

    def phase(self, name, bodies):
        def wrap(e, engobj):
            return _Fence(self, engobj, AUTOSEM[e]) if e in AUTOSEM else engobj

        snap = dict(self.cnt)
        self.mock = True
        for e in ENGS:
            if e in bodies:
                bodies[e](wrap(e, _MockEng()))
        mock_end = dict(self.cnt)
        self.cnt = snap
        self.mock = False
        with self.nc.Block(name, no_gpsimd_drain=False) as blk:
            for e in ENGS:
                if e in bodies:
                    body = bodies[e]
                    getattr(blk, e)(lambda eng, b=body, e2=e: b(wrap(e2, eng)))
        for kk2 in set(mock_end) | set(self.cnt):
            assert mock_end.get(kk2, 0) == self.cnt.get(kk2, 0), (
                f"phase {name}: count drift on {kk2}: mock={mock_end.get(kk2, 0)} "
                f"real={self.cnt.get(kk2, 0)}")

    def build(self):
        nc = self.nc
        dbgkind = "ExternalOutput" if self.dbg else "Internal"
        self.x0 = nc.dram_tensor("x0", [S, D], F32, kind="ExternalInput").ap()
        self.wqk0 = nc.dram_tensor("wqk0", [32, 128, D], F32, kind="ExternalInput").ap()
        self.wv0 = nc.dram_tensor("wv0", [16, 128, D], F32, kind="ExternalInput").ap()
        self.wo0 = nc.dram_tensor("wo0", [NH, 128, D], F32, kind="ExternalInput").ap()
        self.w13 = nc.dram_tensor("w13", [NH, 128, 2 * F0], F32, kind="ExternalInput").ap()
        self.w2p = nc.dram_tensor("w2p", [NH, 128, F0], F32, kind="ExternalInput").ap()
        self.nwbi = nc.dram_tensor("nwbi", [128, F0], F32, kind="ExternalInput").ap()
        self.wq1 = nc.dram_tensor("wq1", [NH, 128, D], F32, kind="ExternalInput").ap()
        self.wk1 = nc.dram_tensor("wk1", [1, 128, D], F32, kind="ExternalInput").ap()
        self.wv1 = nc.dram_tensor("wv1", [1, 128, D], F32, kind="ExternalInput").ap()
        self.wo1 = nc.dram_tensor("wo1", [NH, 128, D], F32, kind="ExternalInput").ap()
        self.wgu1 = nc.dram_tensor("wgu1", [NFT, 128, 2 * D], F32, kind="ExternalInput").ap()
        self.wd1 = nc.dram_tensor("wd1", [NH, 128, F0], F32, kind="ExternalInput").ap()
        self.gsp = nc.dram_tensor("gsp", [128, NFT], F32, kind="ExternalInput").ap()
        self.usp = nc.dram_tensor("usp", [128, NFT], F32, kind="ExternalInput").ap()
        self.dsp = nc.dram_tensor("dsp", [128, NH], F32, kind="ExternalInput").ap()
        self.iden = nc.dram_tensor("iden", [128, 128], BF16, kind="ExternalInput").ap()
        self.mskin = nc.dram_tensor("mskin", [128, 2048], BF16, kind="ExternalInput").ap()
        self.cst = nc.dram_tensor("cst", [128, 8], F32, kind="ExternalInput").ap()

        self.out = nc.dram_tensor("outp", [S, D], F32, kind="ExternalOutput").ap()

        self.codes_h = nc.dram_tensor("codes_h", [NH, 128, S], BF16, kind=dbgkind).ap()
        self.xn_h = nc.dram_tensor("xn_h", [NH, 128, S], BF16, kind=dbgkind).ap()
        self.q_hb = nc.dram_tensor("q_hb", [HEADS, 128, S], BF16, kind=dbgkind).ap()
        self.v_hb = nc.dram_tensor("v_hb", [NT, 128, D], BF16, kind=dbgkind).ap()
        self.s_hb = nc.dram_tensor("s_hb", [NT, 128, D], BF16, kind=dbgkind).ap()
        self.g_hb = nc.dram_tensor("g_hb", [NT, 128, D], BF16, kind=dbgkind).ap()
        self.x1_h = nc.dram_tensor("x1_h", [NT, 128, D], F32, kind=dbgkind).ap()
        self.gn_hb = nc.dram_tensor("gn_hb", [S, F0], BF16, kind=dbgkind).ap()
        self.hf_h = nc.dram_tensor("hf_h", [NFT, 128, S], BF16, kind=dbgkind).ap()

        ctx = self.ctx
        self.P16 = ctx.enter_context(nc.sbuf_tensor("P16", [128, 86016], BF16))
        self.P32 = ctx.enter_context(nc.sbuf_tensor("P32", [128, 8192], F32))
        self.PI = ctx.enter_context(nc.sbuf_tensor("PI", [128, 1024], I32))
        self.ident = ctx.enter_context(nc.sbuf_tensor("identS", [128, 128], BF16))
        self.sinv = ctx.enter_context(nc.sbuf_tensor("sinv", [128, NT], F32))
        self.sinv3 = ctx.enter_context(nc.sbuf_tensor("sinv3", [128, NT], F32))
        self.ssqB = ctx.enter_context(nc.sbuf_tensor("ssqB", [128, NT], F32))
        self.mxgB = ctx.enter_context(nc.sbuf_tensor("mxgB", [128, NT], F32))
        self.aa8 = ctx.enter_context(nc.sbuf_tensor("aa8", [128, NT], F32))
        self.i2v = ctx.enter_context(nc.sbuf_tensor("i2v", [128, NT], F32))
        self.gs_s = ctx.enter_context(nc.sbuf_tensor("gs_s", [128, NFT], F32))
        self.us_s = ctx.enter_context(nc.sbuf_tensor("us_s", [128, NFT], F32))
        self.ds_s = ctx.enter_context(nc.sbuf_tensor("ds_s", [128, NH], F32))
        self.epsb = ctx.enter_context(nc.sbuf_tensor("epsb", [128, 1], F32))
        self.st1 = ctx.enter_context(nc.sbuf_tensor("st1", [128, 8], F32))
        self.st2 = ctx.enter_context(nc.sbuf_tensor("st2", [128, 8], F32))
        self.st3 = ctx.enter_context(nc.sbuf_tensor("st3", [128, 8], F32))

        self.A = self.P16[:, 0:32768]
        self.Bp = self.P16[:, 32768:65536]
        self.Cp = self.P16[:, 65536:86016]

        ph = 1
        self.phase_setup()
        if self.dbg_upto >= ph:
            self.phase_ln(0)
        ph += 1
        if self.dbg_upto >= ph:
            self.phase_qkv(0)
        ph += 1
        if self.dbg_upto >= ph:
            self.phase_att(0)
        ph += 1
        if self.dbg_upto >= ph:
            self.phase_wo(0)
        ph += 1
        if self.dbg_upto >= ph:
            self.phase_ffn0A()
            self.phase_ffn0S()
            self.phase_ffn0B()
        ph += 1
        if self.dbg_upto >= ph:
            self.phase_ln(1)
        ph += 1
        if self.dbg_upto >= ph:
            self.phase_qkv(1)
        ph += 1
        if self.dbg_upto >= ph:
            self.phase_att(1)
        ph += 1
        if self.dbg_upto >= ph:
            self.phase_wo(1)
        ph += 1
        if self.dbg_upto >= ph:
            self.phase_ffn1gu()
            self.phase_ffn1dn()
        ph += 1
        if self.dbg_upto >= ph:
            self.phase_final()
        else:
            self.phase_flush()
        self.ctx.close()
        return self.nc

    def phase_setup(self):
        k = self

        def syncf(sy):
            for dst, src in [(k.ident[:], k.iden), (k.msk(), k.mskin),
                             (k.gs_s[:], k.gsp), (k.us_s[:], k.usp),
                             (k.ds_s[:], k.dsp), (k.st3[:], k.cst)]:
                sy.dma_start(out=dst, in_=src).then_inc(k.sem("ld"), 16)
                k.cnt["ld"] = k.cnt.get("ld", 0) + 16
            k.wtv(sy, "ld", k.cnt.get("ld", 0))
            k.bump(sy.nop(), "sY", mark="setup_done")

        def vecf(v):
            k.wt(v, "setup_done")
            k.bump(v.tensor_copy(k.epsb[:], k.st3[:, 0:1]), "sV", mark="eps_ready")

        self.phase("setup", {"sync": syncf, "vector": vecf})

    def msk(self):
        return self.Cp[:, 18432:20480]

    def phase_ln(self, L):
        k = self
        XF = [k.P32[:, 0:2048], k.P32[:, 2048:4096]]
        XNF = k.P32[:, 4096:6144]
        RS1 = k.P32[:, 6144:7168]
        RS2 = k.P32[:, 7168:8192]
        XN16 = k.Cp[:, 0:2048]
        SC16 = k.Cp[:, 2048:4096]
        CFM = k.Cp[:, 4096:6144]
        SGS = [k.Cp[:, 6144:8192], k.Cp[:, 8192:10240]]
        GGS = [k.Cp[:, 10240:12288], k.Cp[:, 12288:14336]]
        DUMP = k.Cp[:, 14336:16384]
        sums = k.st1[:, 0:1]; ssqv = k.st1[:, 1:2]; mean = k.st1[:, 2:3]
        msq = k.st1[:, 3:4]; varr = k.st1[:, 4:5]; sd = k.st1[:, 5:6]; rstd = k.st1[:, 6:7]
        ssn = k.st1[:, 7:8]
        rmx = k.st2[:, 0:1]; rm2 = k.st2[:, 1:2]; rcp = k.st2[:, 2:3]; rcp127 = k.st2[:, 3:4]
        tmp1 = k.st2[:, 4:5]
        Afm = k.A.rearrange("p (h t) -> p h t", h=NH)

        with self.nc.psum_tensor(f"TP{L}", [128, 2048], BF16) as TP:
            halves = [TP[:, 0:1024], TP[:, 1024:2048]]

            def syncf(sy):
                for i in range(NT):
                    if i >= 2:
                        k.wt(sy, f"ln{L}xfree_{i-2}")
                        if L == 1:
                            k.wtv(sy, "st", k.marks[f"ln{L}stchk_{i-2}"][1]) if False else None
                            k.wt(sy, f"ln{L}stchk_{i-2}")
                    sy.dma_start(out=XF[i % 2][:], in_=k.x0[i * 128:(i + 1) * 128, :]
                                 ).then_inc(k.sem(f"dx{i%2}"), 16)
                    amt = 16
                    if L == 1:
                        sy.dma_start(out=SGS[i % 2][:], in_=k.s_hb[i]).then_inc(k.sem(f"dx{i%2}"), 16)
                        sy.dma_start(out=GGS[i % 2][:], in_=k.g_hb[i]).then_inc(k.sem(f"dx{i%2}"), 16)
                        amt = 48
                    k.bump(None, f"dx{i%2}", amt, mark=f"ln{L}ld_{i}")
                    if L == 1:
                        k.wt(sy, f"ln{L}x1_{i}")
                        sy.dma_start(out=k.x1_h[i], in_=XF[i % 2][:]).then_inc(k.sem("st"), 16)
                        k.bump(None, "st", 16)
                        k.wt(sy, f"ln{L}xnev_{i}_1")
                        sy.dma_start(out=k.xn_h.rearrange("h p t -> p h t")[:, :, i * 128:(i + 1) * 128],
                                     in_=Afm[:, :, i * 128:(i + 1) * 128]).then_inc(k.sem("st"), 16)
                        k.bump(None, "st", 16)
                        k.wtv(sy, "st", k.cnt.get("st", 0))
                        k.bump(sy.nop(), "sY", mark=f"ln{L}stchk_{i}")
                    elif "codes" not in DBG_SKIP:
                        k.wt(sy, f"ln{L}cdev_{i}_1")
                        sy.dma_start(out=k.codes_h.rearrange("h p t -> p h t")[:, :, i * 128:(i + 1) * 128],
                                     in_=CFM.rearrange("p (h t) -> p h t", h=NH)).then_inc(k.sem("st"), 16)
                        k.bump(None, "st", 16)
                        k.wtv(sy, "st", k.cnt.get("st", 0))
                        k.bump(sy.nop(), "sY", mark=f"ln{L}stchk_{i}")
                k.wtv(sy, "st", k.cnt.get("st", 0))
                k.bump(sy.nop(), "sY", mark=f"ln{L}_done")

            def vecf(v):
                for i in range(NT):
                    x = XF[i % 2]
                    k.wt(v, f"ln{L}ld_{i}")
                    if L == 1:
                        v.tensor_tensor(XNF[:], SGS[i % 2][:], GGS[i % 2][:], OP.mult)
                        ins = v.tensor_tensor(x[:], x[:], XNF[:], OP.add)
                        k.bump(ins, "sV", mark=f"ln{L}x1_{i}")
                    ins = v.tensor_reduce(sums, x[:], AX, OP.add)
                    k.bump(ins, "sV", mark=f"ln{L}sum_{i}")
                    k.wt(v, f"ln{L}msq_{i}")
                    v.tensor_tensor(varr, ssn, msq, OP.subtract)
                    ins = v.tensor_tensor(varr, varr, k.epsb[:], OP.add)
                    k.bump(ins, "sV", mark=f"ln{L}var_{i}")
                    k.wt(v, f"ln{L}sd_{i}")
                    ins = v.reciprocal(rstd, sd)
                    k.bump(ins, "sV", mark=f"ln{L}rstd_{i}")
                    k.wt(v, f"ln{L}rstd_{i}")
                    ins = v.tensor_scalar(XNF[:], x[:], mean, rstd, OP.subtract, OP.mult)
                    k.bump(ins, "sV", mark=f"ln{L}xn_{i}")
                    if L == 1:
                        k.marks[f"ln{L}xfree_{i}"] = k.marks[f"ln{L}xn_{i}"]
                    elif "codes" in DBG_SKIP:
                        k.marks[f"ln{L}xfree_{i}"] = k.marks[f"ln{L}xn_{i}"]
                    else:
                        ins = v.tensor_reduce(rmx, XNF[:], AX, OP.max, apply_absolute_value=True)
                        k.bump(ins, "sV", mark=f"ln{L}rmx_{i}")
                        k.wt(v, f"ln{L}rmx_{i}")
                        v.tensor_single_scalar(rm2, rmx, 1e-5, OP.max)
                        ins = v.reciprocal(rcp, rm2)
                        k.bump(ins, "sV", mark=f"ln{L}rcp_{i}")
                        k.wt(v, f"ln{L}rcp_{i}")
                        v.tensor_single_scalar(rcp127, rcp, 127.0, OP.mult)
                        v.tensor_single_scalar(k.sinv[:, i:i + 1], rm2, 1.0 / 127.0, OP.mult)
                        v.tensor_tensor(tmp1, k.sinv[:, i:i + 1], k.sinv[:, i:i + 1], OP.mult)
                        v.tensor_tensor(k.sinv3[:, i:i + 1], tmp1, k.sinv[:, i:i + 1], OP.mult)
                        k.wt(v, f"ln{L}xn16_{i}")
                        for hh in range(2):
                            xnh = XNF[:, hh * 1024:(hh + 1) * 1024]
                            v.tensor_scalar(RS1[:], xnh, rcp127, 0.5, OP.mult, OP.add)
                            v.tensor_copy(k.PI[:], RS1[:])
                            v.tensor_copy(RS2[:], k.PI[:])
                            v.tensor_tensor(xnh, RS2[:], RS1[:], OP.is_gt)
                            ins = v.tensor_tensor(SC16[:, hh * 1024:(hh + 1) * 1024],
                                                  RS2[:], xnh, OP.subtract)
                        k.bump(ins, "sV", mark=f"ln{L}codes_{i}")
                        k.marks[f"ln{L}xfree_{i}"] = k.marks[f"ln{L}codes_{i}"]

            def scaf(sc):
                for i in range(NT):
                    k.wt(sc, f"ln{L}ld_{i}")
                    if L == 1:
                        k.wt(sc, f"ln{L}x1_{i}")
                    ins = sc.activation(DUMP[:], XF[i % 2][:], AF.Square, accum_out=ssqv)
                    k.bump(ins, "sS", mark=f"ln{L}ssq_{i}")
                    k.wt(sc, f"ln{L}sum_{i}")
                    ins = sc.mul(mean, sums, 1.0 / D)
                    k.bump(ins, "sS", mark=f"ln{L}mean_{i}")
                    k.wt(sc, f"ln{L}ssq_{i}")
                    sc.mul(ssn, ssqv, 1.0 / D)
                    k.wt(sc, f"ln{L}mean_{i}")
                    ins = sc.activation(msq, mean, AF.Square)
                    k.bump(ins, "sS", mark=f"ln{L}msq_{i}")
                    k.wt(sc, f"ln{L}var_{i}")
                    ins = sc.activation(sd, varr, AF.Sqrt)
                    k.bump(ins, "sS", mark=f"ln{L}sd_{i}")
                    k.wt(sc, f"ln{L}xn_{i}")
                    ins = sc.copy(XN16[:], XNF[:])
                    k.bump(ins, "sS", mark=f"ln{L}xn16_{i}")
                    if "xpose" not in DBG_SKIP:
                        for b in range(2):
                            k.wt(sc, f"ln{L}txn_{i}_{b}")
                            if "ev3d" in DBG_SKIP:
                                for j in range(8):
                                    ins = sc.copy(k.A[:, (b * 8 + j) * 2048 + i * 128:
                                                     (b * 8 + j) * 2048 + i * 128 + 128],
                                                  halves[b][:, j * 128:(j + 1) * 128])
                            else:
                                dst = Afm[:, b * 8:(b + 1) * 8, i * 128:(i + 1) * 128]
                                ins = sc.copy(dst, halves[b].rearrange("p (h t) -> p h t", h=8))
                            k.bump(ins, "sS", mark=f"ln{L}xnev_{i}_{b}")
                    if L == 0 and "codes" not in DBG_SKIP and "xpose" not in DBG_SKIP:
                        for b in range(2):
                            k.wt(sc, f"ln{L}tcd_{i}_{b}")
                            dst = CFM.rearrange("p (h t) -> p h t", h=NH)[:, b * 8:(b + 1) * 8, :]
                            ins = sc.copy(dst, halves[b].rearrange("p (h t) -> p h t", h=8))
                            k.bump(ins, "sS", mark=f"ln{L}cdev_{i}_{b}")

            def tenf(t):
                if "xpose" in DBG_SKIP:
                    return
                tphist = {}
                for i in range(NT):
                    k.wt(t, f"ln{L}xn16_{i}")
                    for b in range(2):
                        if b in tphist:
                            k.wt(t, tphist[b])
                        for j in range(8):
                            ins = t.transpose(halves[b][:, j * 128:(j + 1) * 128],
                                              XN16[:, (b * 8 + j) * 128:(b * 8 + j) * 128 + 128],
                                              k.ident[:])
                        k.bump(ins, "sT", mark=f"ln{L}txn_{i}_{b}")
                        tphist[b] = f"ln{L}xnev_{i}_{b}"
                    if L == 0 and "codes" not in DBG_SKIP:
                        k.wt(t, f"ln{L}codes_{i}")
                        for b in range(2):
                            k.wt(t, tphist[b])
                            for j in range(8):
                                ins = t.transpose(halves[b][:, j * 128:(j + 1) * 128],
                                                  SC16[:, (b * 8 + j) * 128:(b * 8 + j) * 128 + 128],
                                                  k.ident[:])
                            k.bump(ins, "sT", mark=f"ln{L}tcd_{i}_{b}")
                            tphist[b] = f"ln{L}cdev_{i}_{b}"

            self.phase(f"ln{L}", {"tensor": tenf, "vector": vecf, "scalar": scaf, "sync": syncf})

    def phase_qkv(self, L):
        k = self
        WS = [k.Cp[:, 0:2048], k.Cp[:, 2048:4096]]
        QS = k.Cp[:, 4096:6144]
        VTS = k.Cp[:, 6144:8192]
        if L == 0:
            obs = [("q", i, k.wqk0[i]) for i in range(16)] + \
                  [("k", i, k.wqk0[16 + i]) for i in range(16)] + \
                  [("v", i, k.wv0[i]) for i in range(16)]
        else:
            obs = [("q", i, k.wq1[i]) for i in range(16)] + \
                  [("k", 0, k.wk1[0]), ("v", 0, k.wv1[0])]

        with self.nc.psum_tensor(f"QP{L}", [128, 2048], F32) as PB, \
             self.nc.psum_tensor(f"QT{L}", [128, 2048], BF16) as TP:
            tphalves = [TP[:, 0:1024], TP[:, 1024:2048]]

            def gpsf(g):
                for n, (kind, ob, src) in enumerate(obs):
                    if n >= 2:
                        k.wt(g, f"qkv{L}free_{n-2}")
                    g.dma_start(out=WS[n % 2][:], in_=src).then_inc(k.sem(f"dw{n%2}"), 16)
                    k.bump(None, f"dw{n%2}", 16, mark=f"qkv{L}w_{n}")

            def tenf(t):
                tphist = {}
                for n, (kind, ob, src) in enumerate(obs):
                    k.wt(t, f"qkv{L}w_{n}")
                    for tc in range(4):
                        if n >= 1:
                            k.wt(t, f"qkv{L}ev_{n-1}_{tc}")
                        pq = PB[:, tc * 512:(tc + 1) * 512]
                        for ht in range(NH):
                            ins = t.matmul(pq, WS[n % 2][:, ht * 128:(ht + 1) * 128],
                                           k.A[:, ht * 2048 + tc * 512: ht * 2048 + tc * 512 + 512],
                                           start=(ht == 0), stop=(ht == NH - 1))
                        k.bump(ins, "sT", mark=f"qkv{L}mm_{n}_{tc}")
                    if kind == "v":
                        k.wt(t, f"qkv{L}ev_{n}_3")
                        for half in range(2):
                            if half in tphist:
                                k.wt(t, tphist[half])
                            for j in range(8):
                                tt = half * 8 + j
                                ins = t.transpose(tphalves[half][:, j * 128:(j + 1) * 128],
                                                  QS[:, tt * 128:(tt + 1) * 128], k.ident[:])
                            k.bump(ins, "sT", mark=f"qkv{L}vt_{n}_{half}")
                            tphist[half] = f"qkv{L}vte_{n}_{half}"

            def vecf(v):
                for n, (kind, ob, src) in enumerate(obs):
                    for tc in range(4):
                        k.wt(v, f"qkv{L}mm_{n}_{tc}")
                        pq = PB[:, tc * 512:(tc + 1) * 512]
                        if kind == "q":
                            if n >= 1 and tc == 0:
                                k.wt(v, f"qkv{L}qst_{n-1}") if obs[n - 1][0] == "q" else None
                            ins = v.tensor_single_scalar(QS[:, tc * 512:(tc + 1) * 512], pq, SCALE, OP.mult)
                        elif kind == "k":
                            dst = (k.Bp[:, ob * 2048 + tc * 512: ob * 2048 + tc * 512 + 512]
                                   if L == 0 else k.Bp[:, tc * 512:(tc + 1) * 512])
                            ins = v.tensor_copy(dst, pq)
                        else:
                            if tc == 0:
                                pv = [m for m, (kk, _, _) in enumerate(obs[:n]) if kk in ("q", "v")]
                                if pv:
                                    pn = pv[-1]
                                    if obs[pn][0] == "q":
                                        k.wt(v, f"qkv{L}qst_{pn}")
                                    else:
                                        k.wt(v, f"qkv{L}vt_{pn}_1")
                            ins = v.tensor_copy(QS[:, tc * 512:(tc + 1) * 512], pq)
                        k.bump(ins, "sV", mark=f"qkv{L}ev_{n}_{tc}")
                    k.marks[f"qkv{L}free_{n}"] = k.marks[f"qkv{L}ev_{n}_3"]

            def scaf(sc):
                for n, (kind, ob, src) in enumerate(obs):
                    if kind != "v":
                        continue
                    for half in range(2):
                        k.wt(sc, f"qkv{L}vt_{n}_{half}")
                        prevv = [m for m, (kk, _, _) in enumerate(obs[:n]) if kk == "v"]
                        if prevv:
                            k.wt(sc, f"qkv{L}vstf_{prevv[-1]}_{half}")
                        ins = sc.copy(VTS[:, half * 1024:(half + 1) * 1024], tphalves[half][:])
                        k.bump(ins, "sS", mark=f"qkv{L}vte_{n}_{half}")

            def syncf(sy):
                for n, (kind, ob, src) in enumerate(obs):
                    if kind == "q":
                        k.wt(sy, f"qkv{L}ev_{n}_3")
                        sy.dma_start(out=k.q_hb[ob], in_=QS[:]).then_inc(k.sem("st"), 16)
                        k.bump(None, "st", 16)
                        k.wtv(sy, "st", k.cnt.get("st", 0))
                        k.bump(sy.nop(), "sY", mark=f"qkv{L}qst_{n}")
                    elif kind == "v":
                        for half in range(2):
                            k.wt(sy, f"qkv{L}vte_{n}_{half}")
                            dst = k.v_hb.rearrange("n p t -> p n t")[:, half * 8:(half + 1) * 8,
                                                                     ob * 128:(ob + 1) * 128]
                            srcap = VTS[:, half * 1024:(half + 1) * 1024].rearrange("p (n t) -> p n t", n=8)
                            sy.dma_start(out=dst, in_=srcap).then_inc(k.sem("st"), 16)
                            k.bump(None, "st", 16)
                            k.wtv(sy, "st", k.cnt.get("st", 0))
                            k.bump(sy.nop(), "sY", mark=f"qkv{L}vstf_{n}_{half}")
                k.wtv(sy, "st", k.cnt.get("st", 0))
                k.bump(sy.nop(), "sY", mark=f"qkv{L}_done")

            self.phase(f"qkv{L}", {"tensor": tenf, "vector": vecf, "scalar": scaf,
                                   "sync": syncf, "gpsimd": gpsf})

    def phase_att(self, L):
        k = self
        causal = (L == 0)
        QH = [k.Cp[:, 0:2048], k.Cp[:, 2048:4096]]
        VH = [k.Cp[:, 4096:6144], k.Cp[:, 6144:8192]]
        PSB = k.Cp[:, 8192:10240]
        PT = k.Cp[:, 10240:18432]
        negm = k.st1[:, 0:1]; zz = k.st1[:, 1:2]; rz = k.st1[:, 2:3]

        with self.nc.psum_tensor(f"SC{L}", [128, 2048], F32) as SC, \
             self.nc.psum_tensor(f"PP{L}", [128, 2048], BF16) as PP, \
             self.nc.psum_tensor(f"OT{L}", [128, 512], F32) as OT:
            pph = [PP[:, 0:1024], PP[:, 1024:2048]]

            def nchf(qt):
                return qt // 4 + 1 if causal else 4

            def transp(t, h, qt, ppq):
                nkb = nchf(qt) * 4
                k.wt(t, f"a{L}pn_{h}_{qt}")
                for b in range((nkb + 7) // 8):
                    half = b % 2
                    if half in ppq:
                        k.wt(t, ppq[half])
                    j0, j1 = b * 8, min(nkb, b * 8 + 8)
                    for j in range(j0, j1):
                        ins = t.transpose(pph[half][:, (j - j0) * 128:(j - j0 + 1) * 128],
                                          PSB[:, j * 128:(j + 1) * 128], k.ident[:])
                    k.bump(ins, "sT", mark=f"a{L}tpb_{h}_{qt}_{b}")
                    ppq[half] = f"a{L}pte_{h}_{qt}_{b}"
                k.marks[f"a{L}tp_{h}_{qt}"] = k.marks[f"a{L}tpb_{h}_{qt}_{(nkb-1)//8}"]

            def syncf(sy):
                for h in range(HEADS):
                    if h >= 2:
                        k.wt(sy, f"a{L}hdone_{h-2}")
                    sy.dma_start(out=QH[h % 2][:], in_=k.q_hb[h]).then_inc(k.sem(f"dq{h%2}"), 16)
                    c0 = h * 128 if L == 0 else 0
                    sy.dma_start(out=VH[h % 2][:].rearrange("p (n t) -> p n t", n=16),
                                 in_=k.v_hb.rearrange("n p t -> p n t")[:, :, c0:c0 + 128]
                                 ).then_inc(k.sem(f"dq{h%2}"), 16)
                    k.bump(None, f"dq{h%2}", 32, mark=f"a{L}qv_{h}")

            def tenf(t):
                ppq = {}
                for h in range(HEADS):
                    k.wt(t, f"a{L}qv_{h}")
                    for qt in range(NT):
                        nch = nchf(qt)
                        if qt > 0:
                            k.wt(t, f"a{L}exp_{h}_{qt-1}")
                        elif h > 0:
                            k.wt(t, f"a{L}exp_{h-1}_{NT-1}")
                        for c in range(nch):
                            kbase = h * 2048 if L == 0 else 0
                            ins = t.matmul(SC[:, c * 512:(c + 1) * 512],
                                           QH[h % 2][:, qt * 128:(qt + 1) * 128],
                                           k.Bp[:, kbase + c * 512: kbase + c * 512 + 512],
                                           start=True, stop=True)
                        k.bump(ins, "sT", mark=f"a{L}sc_{h}_{qt}")
                        if qt > 0:
                            transp(t, h, qt - 1, ppq)
                            if (qt - 1) % 4 == 3:
                                self._att_pv(t, L, h, (qt - 1) // 4, causal, VH, PT, OT)
                    transp(t, h, NT - 1, ppq)
                    self._att_pv(t, L, h, 3, causal, VH, PT, OT)

            def vecf(v):
                for h in range(HEADS):
                    for qt in range(NT):
                        nch = nchf(qt)
                        k.wt(v, f"a{L}sc_{h}_{qt}")
                        if causal:
                            off = qt % 4
                            v.tensor_tensor(SC[:, (nch - 1) * 512:nch * 512],
                                            SC[:, (nch - 1) * 512:nch * 512],
                                            k.msk()[:, off * 512:(off + 1) * 512], OP.add)
                        ins = v.tensor_reduce(negm, SC[:, 0:nch * 512], AX, OP.max, negate=True)
                        k.bump(ins, "sV", mark=f"a{L}negm_{h}_{qt}")
                        k.wt(v, f"a{L}exp_{h}_{qt}")
                        ins = v.reciprocal(rz, zz)
                        k.bump(ins, "sV", mark=f"a{L}rz_{h}_{qt}")
                        k.wt(v, f"a{L}rz_{h}_{qt}")
                        ins = v.tensor_scalar_mul(PSB[:, 0:nch * 512], PSB[:, 0:nch * 512], rz)
                        k.bump(ins, "sV", mark=f"a{L}pn_{h}_{qt}")
                        if qt % 4 == 0 and qt >= 4:
                            g = qt // 4 - 1
                            k.wt(v, f"a{L}pv_{h}_{g}")
                            ins = v.tensor_copy(k.A[:, h * 2048 + g * 512: h * 2048 + g * 512 + 512], OT[:])
                            k.bump(ins, "sV", mark=f"a{L}oev_{h}_{g}")
                    k.wt(v, f"a{L}pv_{h}_3")
                    ins = v.tensor_copy(k.A[:, h * 2048 + 3 * 512: h * 2048 + 3 * 512 + 512], OT[:])
                    k.bump(ins, "sV", mark=f"a{L}oev_{h}_3")
                    k.marks[f"a{L}hdone_{h}"] = k.marks[f"a{L}oev_{h}_3"]

            def ptevac(sc, h, qt):
                nkb = nchf(qt) * 4
                if qt % 4 == 0:
                    if qt > 0:
                        k.wt(sc, f"a{L}pv_{h}_{qt//4 - 1}")
                    elif h > 0:
                        k.wt(sc, f"a{L}pv_{h-1}_3")
                for b in range((nkb + 7) // 8):
                    k.wt(sc, f"a{L}tpb_{h}_{qt}_{b}")
                    j0, j1 = b * 8, min(nkb, b * 8 + 8)
                    dst = PT.rearrange("p (n t) -> p n t", n=16)[:, j0:j1,
                                                                 (qt % 4) * 128:(qt % 4) * 128 + 128]
                    src = pph[b % 2][:, 0:(j1 - j0) * 128].rearrange("p (n t) -> p n t", n=j1 - j0)
                    ins = sc.copy(dst, src)
                    k.bump(ins, "sS", mark=f"a{L}pte_{h}_{qt}_{b}")
                k.marks[f"a{L}ptall_{h}_{qt}"] = k.marks[f"a{L}pte_{h}_{qt}_{(nkb-1)//8}"]

            def scaf(sc):
                for h in range(HEADS):
                    for qt in range(NT):
                        nch = nchf(qt)
                        k.wt(sc, f"a{L}negm_{h}_{qt}")
                        if qt > 0:
                            k.wt(sc, f"a{L}tp_{h}_{qt-1}")
                        elif h > 0:
                            k.wt(sc, f"a{L}tp_{h-1}_{NT-1}")
                        ins = sc.activation(PSB[:, 0:nch * 512], SC[:, 0:nch * 512], AF.Exp,
                                            bias=negm, scale=1.0, accum_out=zz)
                        k.bump(ins, "sS", mark=f"a{L}exp_{h}_{qt}")
                        if qt > 0:
                            ptevac(sc, h, qt - 1)
                    ptevac(sc, h, NT - 1)

            self.phase(f"att{L}", {"tensor": tenf, "vector": vecf, "scalar": scaf, "sync": syncf})

    def _att_pv(self, t, L, h, g, causal, VH, PT, OT):
        k = self
        ktn = (4 * g + 4) if causal else 16
        k.wt(t, f"a{L}ptall_{h}_{4*g+3}")
        if not (h == 0 and g == 0):
            k.wt(t, f"a{L}oev_{h}_{g-1}" if g > 0 else f"a{L}oev_{h-1}_3")
        for kt in range(ktn):
            ins = t.matmul(OT[:], VH[h % 2][:, kt * 128:(kt + 1) * 128],
                           PT[:, kt * 512:(kt + 1) * 512],
                           start=(kt == 0), stop=(kt == ktn - 1))
        k.bump(ins, "sT", mark=f"a{L}pv_{h}_{g}")

    def phase_wo(self, L):
        k = self
        wo = k.wo0 if L == 0 else k.wo1
        SST = k.Cp[:, 0:2048]
        with self.nc.psum_tensor(f"WOP{L}", [128, 2048], F32) as PB:
            def gpsf(g):
                for hd in range(NH):
                    g.dma_start(out=k.Bp[:, hd * 2048:(hd + 1) * 2048], in_=wo[hd]
                                ).then_inc(k.sem("cw"), 16)
                    k.cnt["cw"] = k.cnt.get("cw", 0) + 16
                k.wtv(g, "cw", k.cnt.get("cw", 0))
                k.bump(g.nop(), "sGc", mark=f"wo{L}_w")

            def tenf(t):
                k.wt(t, f"wo{L}_w")
                for ts in range(NT):
                    for ec in range(4):
                        idx = ts * 4 + ec
                        if idx >= 4:
                            k.wt(t, f"wo{L}ev_{idx-4}")
                        pq = PB[:, ec * 512:(ec + 1) * 512]
                        for hd in range(NH):
                            ins = t.matmul(pq, k.A[:, hd * 2048 + ts * 128: hd * 2048 + ts * 128 + 128],
                                           k.Bp[:, hd * 2048 + ec * 512: hd * 2048 + ec * 512 + 512],
                                           start=(hd == 0), stop=(hd == NH - 1))
                        k.bump(ins, "sT", mark=f"wo{L}mm_{idx}")

            def vecf(v):
                for ts in range(NT):
                    for ec in range(4):
                        idx = ts * 4 + ec
                        k.wt(v, f"wo{L}mm_{idx}")
                        if ts > 0 and ec == 0:
                            k.wt(v, f"wo{L}sfree_{ts-1}")
                        ins = v.tensor_copy(SST[:, ec * 512:(ec + 1) * 512], PB[:, ec * 512:(ec + 1) * 512])
                        k.bump(ins, "sV", mark=f"wo{L}ev_{idx}")

            def syncf(sy):
                for ts in range(NT):
                    k.wt(sy, f"wo{L}ev_{ts*4+3}")
                    sy.dma_start(out=k.s_hb[ts], in_=SST[:]).then_inc(k.sem("st"), 16)
                    k.bump(None, "st", 16)
                    k.wtv(sy, "st", k.cnt.get("st", 0))
                    k.bump(sy.nop(), "sY", mark=f"wo{L}sfree_{ts}")
                k.bump(sy.nop(), "sY", mark=f"wo{L}_done")

            self.phase(f"wo{L}", {"tensor": tenf, "vector": vecf, "sync": syncf, "gpsimd": gpsf})

    def phase_ffn0A(self):
        k = self
        W1S = [k.A[:, 0:8192], k.A[:, 8192:16384]]
        W3S = [k.A[:, 16384:24576], k.A[:, 24576:32768]]
        NWB = k.Cp[:, 0:5632]
        GNS = [k.Cp[:, 5632:6144], k.Cp[:, 6144:6656]]
        DUMP = k.Cp[:, 6656:7168]
        RL = k.P32[:, 0:512]
        M1 = k.P32[:, 512:1024]
        GT = k.P32[:, 1024:1536]
        ssp = k.st1[:, 0:1]; mxp = k.st1[:, 1:2]

        with self.nc.psum_tensor("F0P", [128, 2048], F32) as PB:
            X1 = [PB[:, 0:512], PB[:, 512:1024]]
            X3 = [PB[:, 1024:1536], PB[:, 1536:2048]]

            def gpsf(g):
                g.dma_start(out=NWB[:], in_=k.nwbi).then_inc(k.sem("cw"), 16)
                k.cnt["cw"] = k.cnt.get("cw", 0) + 16
                k.wtv(g, "cw", k.cnt.get("cw", 0))
                k.bump(g.nop(), "sGc", mark="f0_nwb")
                for fc in range(NFC):
                    if fc >= 2:
                        k.wt(g, f"f0adone_{fc-2}")
                    g.dma_start(out=W1S[fc % 2][:].rearrange("p (h t) -> p h t", h=NH),
                                in_=k.w13[:, :, fc * 512:(fc + 1) * 512].rearrange("h p t -> p h t")
                                ).then_inc(k.sem(f"dw{fc%2}"), 16)
                    g.dma_start(out=W3S[fc % 2][:].rearrange("p (h t) -> p h t", h=NH),
                                in_=k.w13[:, :, F0 + fc * 512:F0 + fc * 512 + 512].rearrange("h p t -> p h t")
                                ).then_inc(k.sem(f"dw{fc%2}"), 16)
                    k.bump(None, f"dw{fc%2}", 32, mark=f"f0w_{fc}")

            def syncf(sy):
                sy.dma_start(out=k.Bp[:].rearrange("p (h t) -> p h t", h=NH),
                             in_=k.codes_h.rearrange("h p t -> p h t")).then_inc(k.sem("ld"), 16)
                k.cnt["ld"] = k.cnt.get("ld", 0) + 16
                k.wtv(sy, "ld", k.cnt.get("ld", 0))
                k.bump(sy.nop(), "sY", mark="f0_codes")
                for fc in range(NFC):
                    for tt in range(NT):
                        n = fc * NT + tt
                        k.wt(sy, f"f0gn_{fc}_{tt}")
                        sy.dma_start(out=k.gn_hb[tt * 128:(tt + 1) * 128, fc * 512:(fc + 1) * 512],
                                     in_=GNS[n % 2][:]).then_inc(k.sem("st"), 16)
                        k.bump(None, "st", 16)
                        k.wtv(sy, "st", k.cnt.get("st", 0))
                        k.bump(sy.nop(), "sY", mark=f"f0gnfree_{fc}_{tt}")
                k.bump(sy.nop(), "sY", mark="f0A_done")

            def tenf(t):
                k.wt(t, "f0_codes")
                for fc in range(NFC):
                    k.wt(t, f"f0w_{fc}")
                    for tt in range(NT):
                        pp = tt % 2
                        n = fc * NT + tt
                        if n >= 2:
                            pf, pt = divmod(n - 2, NT)
                            k.wt(t, f"f0gt_{pf}_{pt}")
                        for ht in range(NH):
                            t.matmul(X1[pp], k.Bp[:, ht * 2048 + tt * 128: ht * 2048 + tt * 128 + 128],
                                     W1S[fc % 2][:, ht * 512:(ht + 1) * 512],
                                     start=(ht == 0), stop=(ht == NH - 1))
                        for ht in range(NH):
                            ins = t.matmul(X3[pp], k.Bp[:, ht * 2048 + tt * 128: ht * 2048 + tt * 128 + 128],
                                           W3S[fc % 2][:, ht * 512:(ht + 1) * 512],
                                           start=(ht == 0), stop=(ht == NH - 1))
                        k.bump(ins, "sT", mark=f"f0mm_{fc}_{tt}")
                    k.marks[f"f0adone_{fc}"] = k.marks[f"f0mm_{fc}_{NT-1}"]

            def scaf(sc):
                for fc in range(NFC):
                    for tt in range(NT):
                        pp = tt % 2
                        n = fc * NT + tt
                        k.wt(sc, f"f0mm_{fc}_{tt}")
                        ins = sc.activation(RL[:], X1[pp], AF.Relu)
                        k.bump(ins, "sS", mark=f"f0rl_{fc}_{tt}")
                        k.wt(sc, f"f0gt_{fc}_{tt}")
                        if n >= 1:
                            pf, pt = divmod(n - 1, NT)
                            k.wt(sc, f"f0ssacc_{pf}_{pt}")
                        ins = sc.activation(DUMP[:], GT[:], AF.Square, accum_out=ssp)
                        k.bump(ins, "sS", mark=f"f0ss_{fc}_{tt}")

            def vecf(v):
                k.wt(v, "f0_nwb")
                for fc in range(NFC):
                    for tt in range(NT):
                        pp = tt % 2
                        n = fc * NT + tt
                        k.wt(v, f"f0rl_{fc}_{tt}")
                        v.tensor_tensor(M1[:], X1[pp], RL[:], OP.mult)
                        ins = v.scalar_tensor_tensor(GT[:], M1[:], k.sinv3[:, tt:tt + 1], X3[pp],
                                                     OP.mult, OP.mult)
                        k.bump(ins, "sV", mark=f"f0gt_{fc}_{tt}")
                        if n >= 2:
                            pf, pt = divmod(n - 2, NT)
                            k.wt(v, f"f0gnfree_{pf}_{pt}")
                        ins = v.tensor_tensor(GNS[n % 2][:], GT[:], NWB[:, fc * 512:(fc + 1) * 512], OP.mult)
                        k.bump(ins, "sV", mark=f"f0gn_{fc}_{tt}")
                        ins = v.tensor_reduce(mxp, GNS[n % 2][:], AX, OP.max, apply_absolute_value=True)
                        k.bump(ins, "sV", mark=f"f0mxr_{fc}_{tt}")
                        k.wt(v, f"f0ss_{fc}_{tt}")
                        k.wt(v, f"f0mxr_{fc}_{tt}")
                        if fc == 0:
                            v.tensor_copy(k.ssqB[:, tt:tt + 1], ssp)
                            ins = v.tensor_copy(k.mxgB[:, tt:tt + 1], mxp)
                        else:
                            v.tensor_tensor(k.ssqB[:, tt:tt + 1], k.ssqB[:, tt:tt + 1], ssp, OP.add)
                            ins = v.tensor_tensor(k.mxgB[:, tt:tt + 1], k.mxgB[:, tt:tt + 1], mxp, OP.max)
                        k.bump(ins, "sV", mark=f"f0ssacc_{fc}_{tt}")

            self.phase("ffn0A", {"tensor": tenf, "vector": vecf, "scalar": scaf,
                                 "sync": syncf, "gpsimd": gpsf})

    def phase_ffn0S(self):
        k = self
        SD16 = k.P32[:, 0:16]
        RSQ = k.P32[:, 16:32]
        CLP = k.P32[:, 32:48]
        REC = k.P32[:, 48:64]

        def scaf(sc):
            k.wt(sc, "f0A_vecdone")
            ins = sc.activation(SD16[:], k.ssqB[:], AF.Sqrt, bias=k.epsb[:], scale=1.0 / F0)
            k.bump(ins, "sS", mark="f0_sd")

        def vecf(v):
            k.bump(v.nop(), "sV", mark="f0A_vecdone")
            k.wt(v, "f0_sd")
            ins = v.reciprocal(RSQ[:], SD16[:])
            k.bump(ins, "sV", mark="f0_rsq")
            k.wt(v, "f0_rsq")
            v.tensor_tensor(CLP[:], RSQ[:], k.mxgB[:], OP.mult)
            v.tensor_single_scalar(CLP[:], CLP[:], 1e-5, OP.max)
            ins = v.reciprocal(REC[:], CLP[:])
            k.bump(ins, "sV", mark="f0_rec")
            k.wt(v, "f0_rec")
            v.tensor_single_scalar(k.i2v[:], CLP[:], 1.0 / 127.0, OP.mult)
            v.tensor_tensor(k.aa8[:], RSQ[:], REC[:], OP.mult)
            ins = v.tensor_single_scalar(k.aa8[:], k.aa8[:], 127.0, OP.mult)
            k.bump(ins, "sV", mark="f0_stats")

        self.phase("ffn0S", {"scalar": scaf, "vector": vecf})

    def phase_ffn0B(self):
        k = self
        iq2 = [k.Bp[:, ft * 1024:(ft + 1) * 1024] for ft in range(32)] + \
              [k.A[:, 20480 + (ft - 32) * 1024: 20480 + (ft - 31) * 1024] for ft in range(32, NFT)]
        GNL = [k.Cp[:, 0:512], k.Cp[:, 512:1024]]
        IQS = k.Cp[:, 1024:1536]
        GF = k.Cp[:, 1536:2560]
        GS = k.Cp[:, 2560:3584]
        W2L = [k.A[:, 0:5632], k.A[:, 5632:11264]]
        YY = k.P32[:, 0:512]
        FF = k.P32[:, 512:1024]
        CMP = k.P32[:, 1024:1536]

        with self.nc.psum_tensor("F0D", [128, 2048], F32) as DN, \
             self.nc.psum_tensor("F0T", [128, 2048], BF16) as TP, \
             self.nc.psum_tensor("F0T2", [128, 1024], BF16) as TP2:
            tph = [TP[:, 0:1024], TP[:, 1024:2048]]

            def syncf(sy):
                for tb in range(2):
                    for tt8 in range(8):
                        for fc in range(NFC):
                            n = (tb * 8 + tt8) * NFC + fc
                            if n >= 2:
                                k.wt(sy, f"f0y_{n-2}")
                            tt = tb * 8 + tt8
                            sy.dma_start(out=GNL[n % 2][:],
                                         in_=k.gn_hb[tt * 128:(tt + 1) * 128, fc * 512:(fc + 1) * 512]
                                         ).then_inc(k.sem(f"dx{n%2}"), 16)
                            k.bump(None, f"dx{n%2}", 16, mark=f"f0gnl_{n}")
                    for dblk in range(NH):
                        k.wt(sy, f"f0gse_{tb}_{dblk}")
                        for b in range(8):
                            tsub = tb * 8 + b
                            sy.dma_start(out=k.g_hb[tsub][:, dblk * 128:(dblk + 1) * 128],
                                         in_=GS[:, b * 128:(b + 1) * 128]).then_inc(k.sem("st"), 16)
                            k.bump(None, "st", 16)
                        k.wtv(sy, "st", k.cnt.get("st", 0))
                        k.bump(sy.nop(), "sY", mark=f"f0gsfree_{tb}_{dblk}")
                k.wtv(sy, "st", k.cnt.get("st", 0))
                k.bump(sy.nop(), "sY", mark="f0B_done")

            def gpsf(g):
                for tb in range(2):
                    for dblk in range(NH):
                        n = tb * NH + dblk
                        if n >= 2:
                            pn = n - 2
                            k.wt(g, f"f0dn_{pn//NH}_{pn%NH}")
                        g.dma_start(out=W2L[n % 2][:], in_=k.w2p[dblk]).then_inc(k.sem(f"dw{n%2}"), 16)
                        k.bump(None, f"dw{n%2}", 16, mark=f"f0w2_{tb}_{dblk}")

            def vecf(v):
                for tb in range(2):
                    for tt8 in range(8):
                        tt = tb * 8 + tt8
                        for fc in range(NFC):
                            n = (tb * 8 + tt8) * NFC + fc
                            k.wt(v, f"f0gnl_{n}")
                            v.tensor_scalar(YY[:], GNL[n % 2][:], k.aa8[:, tt:tt + 1], 0.5,
                                            OP.mult, OP.add)
                            ins = v.tensor_copy(k.PI[:, 0:512], YY[:])
                            k.bump(ins, "sV", mark=f"f0y_{n}")
                            v.tensor_copy(FF[:], k.PI[:, 0:512])
                            v.tensor_tensor(CMP[:], FF[:], YY[:], OP.is_gt)
                            v.tensor_tensor(FF[:], FF[:], CMP[:], OP.subtract)
                            if n >= 1:
                                k.wt(v, f"f0tiq_{n-1}")
                            ins = v.tensor_scalar_mul(IQS[:], FF[:], k.i2v[:, tt:tt + 1])
                            k.bump(ins, "sV", mark=f"f0iq_{n}")

            def tenf(t):
                for tb in range(2):
                    for tt8 in range(8):
                        for fc in range(NFC):
                            n = (tb * 8 + tt8) * NFC + fc
                            k.wt(t, f"f0iq_{n}")
                            if n >= 2:
                                k.wt(t, f"f0iqe_{n-2}")
                            for b in range(4):
                                ins = t.transpose(tph[n % 2][:, b * 128:(b + 1) * 128],
                                                  IQS[:, b * 128:(b + 1) * 128], k.ident[:])
                            k.bump(ins, "sT", mark=f"f0tiq_{n}")
                    for dblk in range(NH):
                        nn = tb * NH + dblk
                        k.wt(t, f"f0w2_{tb}_{dblk}")
                        k.wt(t, f"f0iqe_{(tb*8+7)*NFC + NFC - 1}")
                        if nn >= 2:
                            pn = nn - 2
                            k.wt(t, f"f0sil_{pn//NH}_{pn%NH}")
                        half = nn % 2
                        for ft in range(NFT):
                            for tc in range(2):
                                ins = t.matmul(DN[:, half * 1024 + tc * 512: half * 1024 + tc * 512 + 512],
                                               W2L[nn % 2][:, ft * 128:(ft + 1) * 128],
                                               iq2[ft][:, tc * 512:(tc + 1) * 512],
                                               start=(ft == 0), stop=(ft == NFT - 1))
                        k.bump(ins, "sT", mark=f"f0dn_{tb}_{dblk}")
                        k.wt(t, f"f0sil_{tb}_{dblk}")
                        if nn >= 1:
                            pn = nn - 1
                            k.wt(t, f"f0gte_{pn//NH}_{pn%NH}")
                        for b in range(8):
                            ins = t.transpose(TP2[:, (b % 8) * 128:(b % 8) * 128 + 128],
                                              GF[:, b * 128:(b + 1) * 128], k.ident[:])
                        k.bump(ins, "sT", mark=f"f0tg_{tb}_{dblk}")

            def scaf(sc):
                for tb in range(2):
                    for tt8 in range(8):
                        tt = tb * 8 + tt8
                        for fc in range(NFC):
                            n = (tb * 8 + tt8) * NFC + fc
                            k.wt(sc, f"f0tiq_{n}")
                            if tb == 1:
                                k.wt(sc, "f0dn_0_15")
                            ins = None
                            for b in range(4):
                                ft = fc * 4 + b
                                ins = sc.copy(iq2[ft][:, tt8 * 128:(tt8 + 1) * 128],
                                              tph[n % 2][:, b * 128:(b + 1) * 128])
                            k.bump(ins, "sS", mark=f"f0iqe_{n}")
                    for dblk in range(NH):
                        nn = tb * NH + dblk
                        k.wt(sc, f"f0dn_{tb}_{dblk}")
                        half = nn % 2
                        ins = sc.activation(GF[:], DN[:, half * 1024:half * 1024 + 1024], AF.Silu)
                        k.bump(ins, "sS", mark=f"f0sil_{tb}_{dblk}")
                        k.wt(sc, f"f0tg_{tb}_{dblk}")
                        if nn >= 1:
                            pn = nn - 1
                            k.wt(sc, f"f0gsfree_{pn//NH}_{pn%NH}")
                        ins = sc.copy(GS[:], TP2[:])
                        k.bump(ins, "sS", mark=f"f0gte_{tb}_{dblk}")
                        k.marks[f"f0gse_{tb}_{dblk}"] = k.marks[f"f0gte_{tb}_{dblk}"]

            self.phase("ffn0B", {"tensor": tenf, "vector": vecf, "scalar": scaf,
                                 "sync": syncf, "gpsimd": gpsf})

    def phase_ffn1gu(self):
        k = self
        WGU = [k.Cp[:, i * 4096:(i + 1) * 4096] for i in range(4)]
        WG = [WGU[i][:, 0:2048] for i in range(4)]
        WU = [WGU[i][:, 2048:4096] for i in range(4)]
        GG = k.P32[:, 0:2048]
        UU = k.P32[:, 2048:4096]
        HF = k.Cp[:, 16384:18432]

        with self.nc.psum_tensor("F1G", [128, 2048], F32) as GP, \
             self.nc.psum_tensor("F1U", [128, 2048], F32) as UP:
            def syncf(sy):
                sy.dma_start(out=k.Bp[:].rearrange("p (h t) -> p h t", h=NH),
                             in_=k.xn_h.rearrange("h p t -> p h t")).then_inc(k.sem("ld"), 16)
                k.cnt["ld"] = k.cnt.get("ld", 0) + 16
                k.wtv(sy, "ld", k.cnt.get("ld", 0))
                k.bump(sy.nop(), "sY", mark="f1_xn")
                for ft in range(NFT):
                    k.wt(sy, f"f1h_{ft}")
                    sy.dma_start(out=k.hf_h[ft], in_=HF[:]).then_inc(k.sem("st"), 16)
                    k.bump(None, "st", 16)
                    k.wtv(sy, "st", k.cnt.get("st", 0))
                    k.bump(sy.nop(), "sY", mark=f"f1hfree_{ft}")
                k.bump(sy.nop(), "sY", mark="f1gu_done")

            def gpsf(g):
                for ft in range(NFT):
                    if ft >= 4:
                        k.wt(g, f"f1mm_{ft-4}")
                    g.dma_start(out=WGU[ft % 4][:], in_=k.wgu1[ft]).then_inc(k.sem(f"dw{ft%4}"), 16)
                    k.bump(None, f"dw{ft%4}", 16, mark=f"f1w_{ft}")

            def tenf(t):
                k.wt(t, "f1_xn")
                for ft in range(NFT):
                    k.wt(t, f"f1w_{ft}")
                    if ft >= 1:
                        k.wt(t, f"f1ue_{ft-1}")
                        k.wt(t, f"f1gg_{ft-1}")
                    for ht in range(NH):
                        for tc in range(4):
                            t.matmul(GP[:, tc * 512:(tc + 1) * 512],
                                     WG[ft % 4][:, ht * 128:(ht + 1) * 128],
                                     k.Bp[:, ht * 2048 + tc * 512: ht * 2048 + tc * 512 + 512],
                                     start=(ht == 0), stop=(ht == NH - 1))
                    for ht in range(NH):
                        for tc in range(4):
                            ins = t.matmul(UP[:, tc * 512:(tc + 1) * 512],
                                           WU[ft % 4][:, ht * 128:(ht + 1) * 128],
                                           k.Bp[:, ht * 2048 + tc * 512: ht * 2048 + tc * 512 + 512],
                                           start=(ht == 0), stop=(ht == NH - 1))
                    k.bump(ins, "sT", mark=f"f1mm_{ft}")

            def scaf(sc):
                for ft in range(NFT):
                    k.wt(sc, f"f1mm_{ft}")
                    if ft >= 1:
                        k.wt(sc, f"f1h_{ft-1}")
                    ins = sc.activation(GG[:], GP[:], AF.Silu, scale=k.gs_s[:, ft:ft + 1])
                    k.bump(ins, "sS", mark=f"f1gg_{ft}")

            def vecf(v):
                for ft in range(NFT):
                    k.wt(v, f"f1mm_{ft}")
                    ins = v.tensor_scalar_mul(UU[:], UP[:], k.us_s[:, ft:ft + 1])
                    k.bump(ins, "sV", mark=f"f1ue_{ft}")
                    k.wt(v, f"f1gg_{ft}")
                    if ft >= 1:
                        k.wt(v, f"f1hfree_{ft-1}")
                    ins = v.tensor_tensor(HF[:], GG[:], UU[:], OP.mult)
                    k.bump(ins, "sV", mark=f"f1h_{ft}")

            self.phase("ffn1gu", {"tensor": tenf, "vector": vecf, "scalar": scaf,
                                  "sync": syncf, "gpsimd": gpsf})

    def phase_ffn1dn(self):
        k = self
        hfm = [k.Bp[:, ft * 1024:(ft + 1) * 1024] for ft in range(32)] + \
              [k.A[:, 20480 + (ft - 32) * 1024: 20480 + (ft - 31) * 1024] for ft in range(32, NFT)]
        WD = [k.A[:, 0:5632], k.A[:, 5632:11264]]
        GF = k.Cp[:, 0:1024]
        GS = k.Cp[:, 1024:2048]

        with self.nc.psum_tensor("F1D", [128, 2048], F32) as DN, \
             self.nc.psum_tensor("F1T", [128, 1024], BF16) as TP:
            def syncf(sy):
                for tb in range(2):
                    if tb == 1:
                        k.wt(sy, "f1dn_0_15")
                    for ft in range(NFT):
                        n = tb * NFT + ft
                        sy.dma_start(out=hfm[ft][:], in_=k.hf_h[ft][:, tb * 1024:(tb + 1) * 1024]
                                     ).then_inc(k.sem(f"dx{n%2}"), 16)
                        k.bump(None, f"dx{n%2}", 16)
                    k.wtv(sy, "dx0", k.cnt.get("dx0", 0))
                    k.wtv(sy, "dx1", k.cnt.get("dx1", 0))
                    k.bump(sy.nop(), "sY", mark=f"f1hl_all_{tb}")
                    for dblk in range(NH):
                        k.wt(sy, f"f1gte_{tb}_{dblk}")
                        for b in range(8):
                            tsub = tb * 8 + b
                            sy.dma_start(out=k.g_hb[tsub][:, dblk * 128:(dblk + 1) * 128],
                                         in_=GS[:, b * 128:(b + 1) * 128]).then_inc(k.sem("st"), 16)
                            k.bump(None, "st", 16)
                        k.wtv(sy, "st", k.cnt.get("st", 0))
                        k.bump(sy.nop(), "sY", mark=f"f1gsfree_{tb}_{dblk}")
                k.wtv(sy, "st", k.cnt.get("st", 0))
                k.bump(sy.nop(), "sY", mark="f1dn_done")

            def gpsf(g):
                for tb in range(2):
                    for dblk in range(NH):
                        n = tb * NH + dblk
                        if n >= 2:
                            pn = n - 2
                            k.wt(g, f"f1dn_{pn//NH}_{pn%NH}")
                        g.dma_start(out=WD[n % 2][:], in_=k.wd1[dblk]).then_inc(k.sem(f"dw{n%2}"), 16)
                        k.bump(None, f"dw{n%2}", 16, mark=f"f1wd_{tb}_{dblk}")

            def tenf(t):
                for tb in range(2):
                    k.wt(t, f"f1hl_all_{tb}")
                    for dblk in range(NH):
                        nn = tb * NH + dblk
                        k.wt(t, f"f1wd_{tb}_{dblk}")
                        if nn >= 2:
                            pn = nn - 2
                            k.wt(t, f"f1sil_{pn//NH}_{pn%NH}")
                        half = nn % 2
                        for ft in range(NFT):
                            for tc in range(2):
                                ins = t.matmul(DN[:, half * 1024 + tc * 512: half * 1024 + tc * 512 + 512],
                                               WD[nn % 2][:, ft * 128:(ft + 1) * 128],
                                               hfm[ft][:, tc * 512:(tc + 1) * 512],
                                               start=(ft == 0), stop=(ft == NFT - 1))
                        k.bump(ins, "sT", mark=f"f1dn_{tb}_{dblk}")
                        k.wt(t, f"f1sil_{tb}_{dblk}")
                        if nn >= 1:
                            pn = nn - 1
                            k.wt(t, f"f1gte_{pn//NH}_{pn%NH}")
                        for b in range(8):
                            ins = t.transpose(TP[:, b * 128:(b + 1) * 128],
                                              GF[:, b * 128:(b + 1) * 128], k.ident[:])
                        k.bump(ins, "sT", mark=f"f1tg_{tb}_{dblk}")

            def scaf(sc):
                for tb in range(2):
                    for dblk in range(NH):
                        nn = tb * NH + dblk
                        k.wt(sc, f"f1dn_{tb}_{dblk}")
                        half = nn % 2
                        ins = sc.activation(GF[:], DN[:, half * 1024:half * 1024 + 1024],
                                            AF.Silu, scale=k.ds_s[:, dblk:dblk + 1])
                        k.bump(ins, "sS", mark=f"f1sil_{tb}_{dblk}")
                        k.wt(sc, f"f1tg_{tb}_{dblk}")
                        if nn >= 1:
                            pn = nn - 1
                            k.wt(sc, f"f1gsfree_{pn//NH}_{pn%NH}")
                        ins = sc.copy(GS[:], TP[:])
                        k.bump(ins, "sS", mark=f"f1gte_{tb}_{dblk}")

            self.phase("ffn1dn", {"tensor": tenf, "scalar": scaf, "sync": syncf, "gpsimd": gpsf})

    def phase_final(self):
        k = self
        XF = [k.P32[:, 0:2048], k.P32[:, 2048:4096]]
        SGF = k.P32[:, 4096:6144]
        SGS = [k.Cp[:, 0:2048], k.Cp[:, 2048:4096]]
        GGS = [k.Cp[:, 4096:6144], k.Cp[:, 6144:8192]]

        def syncf(sy):
            for i in range(NT):
                sy.dma_start(out=XF[i % 2][:], in_=k.x1_h[i]).then_inc(k.sem(f"dx{i%2}"), 16)
                sy.dma_start(out=SGS[i % 2][:], in_=k.s_hb[i]).then_inc(k.sem(f"dx{i%2}"), 16)
                sy.dma_start(out=GGS[i % 2][:], in_=k.g_hb[i]).then_inc(k.sem(f"dx{i%2}"), 16)
                k.bump(None, f"dx{i%2}", 48, mark=f"fin_ld_{i}")
                k.wt(sy, f"fin_x2_{i}")
                sy.dma_start(out=k.out[i * 128:(i + 1) * 128, :], in_=XF[i % 2][:]
                             ).then_inc(k.sem("out"), 16)
                k.cnt["out"] = k.cnt.get("out", 0) + 16
                k.wtv(sy, "out", k.cnt.get("out", 0))
            k.wtv(sy, "out", k.cnt.get("out", 0))

        def vecf(v):
            for i in range(NT):
                k.wt(v, f"fin_ld_{i}")
                v.tensor_tensor(SGF[:], SGS[i % 2][:], GGS[i % 2][:], OP.mult)
                ins = v.tensor_tensor(XF[i % 2][:], XF[i % 2][:], SGF[:], OP.add)
                k.bump(ins, "sV", mark=f"fin_x2_{i}")

        self.phase("final", {"vector": vecf, "sync": syncf})

    def phase_flush(self):
        k = self

        def syncf(sy):
            if k.cnt.get("st", 0):
                k.wtv(sy, "st", k.cnt.get("st", 0))

        self.phase("flush", {"sync": syncf})


def _prep_inputs(ins):
    f32 = np.float32
    wqkvT = np.ascontiguousarray(ins["L0_attn_w_qkv"].astype(f32).T)
    woT0 = np.ascontiguousarray(ins["L0_attn_w_o"].astype(f32).T)
    w13T = np.ascontiguousarray(ins["L0_ffn_w13"].astype(f32).T)
    w2T = np.ascontiguousarray(ins["L0_ffn_w2"].astype(f32).T)
    qwT = np.ascontiguousarray(ins["L1_attn_q_w"].astype(f32).T)
    kwT = np.ascontiguousarray(ins["L1_attn_k_w"].astype(f32).T)
    vwT = np.ascontiguousarray(ins["L1_attn_v_w"].astype(f32).T)
    owT = np.ascontiguousarray(ins["L1_attn_o_w"].astype(f32).T)
    gwT = np.ascontiguousarray(ins["L1_ffn_gate_w"].astype(f32).T)
    uwT = np.ascontiguousarray(ins["L1_ffn_up_w"].astype(f32).T)
    dwT = np.ascontiguousarray(ins["L1_ffn_down_w"].astype(f32).T)

    d = {}
    d["wqk0"] = pack_lhsT(wqkvT[:, 0:4096])
    d["wv0"] = pack_lhsT(wqkvT[:, 4096:6144])
    d["wo0"] = pack_rhs(woT0)
    d["w13"] = pack_rhs(w13T)
    d["w2p"] = pack_lhsT(w2T)
    d["nwbi"] = np.ascontiguousarray(
        np.broadcast_to(ins["L0_ffn_norm_w"].astype(f32)[None, :], (128, F0)))
    d["wq1"] = pack_lhsT(qwT)
    d["wk1"] = pack_lhsT(kwT)
    d["wv1"] = pack_lhsT(vwT)
    d["wo1"] = pack_rhs(owT)
    d["wgu1"] = np.ascontiguousarray(
        np.concatenate([pack_lhsT(gwT), pack_lhsT(uwT)], axis=2))
    d["wd1"] = pack_lhsT(dwT)
    d["gsp"] = np.ascontiguousarray(ins["L1_ffn_gate_s"].astype(f32).reshape(NFT, 128).T)
    d["usp"] = np.ascontiguousarray(ins["L1_ffn_up_s"].astype(f32).reshape(NFT, 128).T)
    d["dsp"] = np.ascontiguousarray(ins["L1_ffn_down_s"].astype(f32).reshape(NH, 128).T)
    d["iden"] = np.eye(128, dtype=BF)
    d["mskin"] = make_masks()
    cst = np.zeros((128, 8), np.float32)
    cst[:, 0] = 1e-5
    d["cst"] = cst
    return d


def kernel(**inputs):
    shared = _prep_inputs(inputs)
    x_all = np.ascontiguousarray(inputs["L0_attn_hidden_states"].astype(np.float32))
    kb = KB()
    nc = kb.build()
    in_maps = []
    for b in range(8):
        m = dict(shared)
        m["x0"] = x_all[b]
        in_maps.append(m)
    res = run_bass_kernel_spmd(nc, in_maps, core_ids=list(range(8)))
    return np.stack([r["outp"] for r in res.results], axis=0).astype(np.float32)
